# version 08 (final)
# baseline (speedup 1.0000x reference)
import numpy as np
import ml_dtypes
import concourse.bass as bass
import concourse.mybir as mybir
from concourse.bass_utils import run_bass_kernel_spmd

F32 = mybir.dt.float32
BF16 = mybir.dt.bfloat16
AF = mybir.ActivationFunctionType
ALU = mybir.AluOpType
AX = mybir.AxisListType
BF = ml_dtypes.bfloat16

D = 1024
DC = 8
TOK = 2048
NG = 4
T = 512
TC = 4
LN_EPS = 1e-5
DEN_EPS = 1e-6
NH, HD, SRC, BATCH = 16, 64, 5, 4

ENGS = ("t", "v", "s", "g", "y")
CTRS = ("pe", "dve", "act", "pool", "ld", "wld", "st")
_OWN = {"t": "pe", "v": "dve", "s": "act", "g": "pool"}


class Prog:

    def __init__(self):
        self.streams = {e: [] for e in ENGS}
        self.c = {n: 0 for n in CTRS}
        self._last_bar = {n: 0 for n in CTRS}

    def emit(self, eng, fn, inc=None, inc_n=1, waits=()):
        if inc is None and eng in _OWN:
            pass
        w = [(n, v) for (n, v) in waits if v > 0]
        self.streams[eng].append((fn, w, inc, inc_n))
        if inc is not None:
            self.c[inc] += inc_n
            return self.c[inc]
        return None

    def t(self, fn, **kw):
        return self.emit("t", fn, inc="pe", **kw)

    def v(self, fn, **kw):
        return self.emit("v", fn, inc="dve", **kw)

    def s(self, fn, **kw):
        return self.emit("s", fn, inc="act", **kw)

    def g(self, fn, **kw):
        return self.emit("g", fn, inc="pool", **kw)

    def barrier(self):
        vals = [(n, self.c[n]) for n in CTRS if self.c[n] > 0]

        def mk(eng):
            own = _OWN.get(eng)
            w = [(n, v) for (n, v) in vals if n != own]
            self.streams[eng].append((None, w, None, 0))
        for eng in ENGS:
            mk(eng)
        self._last_bar = dict(self.c)

    def finalize(self, nc, block, sems):
        waited = {n: set() for n in CTRS}
        for eng in ENGS:
            for fn, waits, inc, inc_n in self.streams[eng]:
                for (n, val) in waits:
                    waited[n].add(val)
        for n in CTRS:
            if self.c[n] > 0:
                waited[n].add(self.c[n])
        for eng in ENGS:
            entries = self.streams[eng]
            fire = [0] * len(entries)
            pend = {}
            run = {n: 0 for n in CTRS}
            for idx, (fn, waits, inc, inc_n) in enumerate(entries):
                if waits:
                    for n, (amt, li) in list(pend.items()):
                        fire[li] += amt
                        del pend[n]
                if fn is None or inc is None:
                    continue
                run[inc] += inc_n
                if inc in ("ld", "wld", "st"):
                    fire[idx] += inc_n
                    continue
                amt, _ = pend.get(inc, (0, None))
                amt += inc_n
                if run[inc] in waited[inc]:
                    fire[idx] += amt
                    pend.pop(inc, None)
                else:
                    pend[inc] = (amt, idx)
            for n, (amt, li) in pend.items():
                fire[li] += amt

            sec = {"t": block.tensor, "v": block.vector, "s": block.scalar,
                   "g": block.gpsimd, "y": block.sync}[eng]

            def body(handle, eng=eng, entries=entries, fire=fire):
                for idx, (fn, waits, inc, inc_n) in enumerate(entries):
                    for (n, val) in waits:
                        handle.wait_ge(sems[n], val)
                    if fn is None:
                        continue
                    inst = fn(handle)
                    if fire[idx]:
                        inst.then_inc(sems[inc], fire[idx])
            sec(body)


def _cm(ctx, *mgrs):
    return [ctx.enter_context(m) for m in mgrs]



def build_l0():
    import os as _os0
    _L0UP = int(_os0.environ.get("L0_UPTO", "9999"))
    nc = bass.Bass("TRN2", target_bir_lowering=False)

    x_in = nc.dram_tensor("x_fm", [D, TOK], F32, kind="ExternalInput").ap()
    nonce_in = nc.dram_tensor("nonce", [1, 4], mybir.dt.int32,
                              kind="ExternalInput").ap()
    xnd_t = nc.dram_tensor("xn_d", [NG * 128, DC * T], BF16)
    kfmd_t = nc.dram_tensor("kfm_d", [NG * 128, DC * T], BF16)
    vtmd_t = nc.dram_tensor("vtm_d", [NG * 128, TC * D], BF16)
    spill_t = nc.dram_tensor("spill_d", [3 * 128, DC * D], F32)
    kspill_t = nc.dram_tensor("kspill_d", [3 * 128, DC], F32)
    smask_t = nc.dram_tensor("smask_d", [128, DC * D], F32)
    kmask_t = nc.dram_tensor("kmask_d", [128, DC], F32)
    shS_t = nc.dram_tensor("shS", [2 * 128, DC * D], F32, addr_space="Shared")
    shK_t = nc.dram_tensor("shK", [2 * 128, DC], F32, addr_space="Shared")
    shF_t = nc.dram_tensor("shF", [2, 16], mybir.dt.int32,
                           addr_space="Shared")
    qkvw_in = nc.dram_tensor("qkv_w", [D, 3 * D], F32, kind="ExternalInput").ap()
    gatew_in = nc.dram_tensor("gate_w", [D, D], F32, kind="ExternalInput").ap()
    w0t_in = nc.dram_tensor("w0t", [D, D], F32, kind="ExternalInput").ap()
    qkvb_c_in = nc.dram_tensor("qkv_b_c", [128, 24], F32, kind="ExternalInput").ap()
    qkvb_r_in = nc.dram_tensor("qkv_b_r", [1, 3 * D], F32, kind="ExternalInput").ap()
    gateb_c_in = nc.dram_tensor("gate_b_c", [128, 8], F32, kind="ExternalInput").ap()
    lng_c_in = nc.dram_tensor("ln_g_c", [128, 8], F32, kind="ExternalInput").ap()
    lnb_c_in = nc.dram_tensor("ln_b_c", [128, 8], F32, kind="ExternalInput").ap()
    hmask_in = nc.dram_tensor("hmask", [128, 1], F32, kind="ExternalInput").ap()
    ident_in = nc.dram_tensor("ident16", [128, 128], BF16, kind="ExternalInput").ap()
    triu_in = nc.dram_tensor("triu", [128, 128], BF16, kind="ExternalInput").ap()
    onesr_in = nc.dram_tensor("ones_r", [1, 128], F32, kind="ExternalInput").ap()
    cpool_in = nc.dram_tensor("cpool", [128, 8], F32, kind="ExternalInput").ap()

    x2_out = nc.dram_tensor("x2", [D, TOK], F32, kind="ExternalOutput").ap()

    from contextlib import ExitStack
    ctx = ExitStack()
    with ctx:
        (qkvw, qkvb_cs,
         qkvb_rs_bf, gateb_cs, lng_cs, lnb_cs, hmask_s, ident, triu, onesr,
         onesr_bf, onesc, onesc_bf, cpool, nonce_sb, flag_sb) = _cm(
            ctx,
            nc.sbuf_tensor("qkvw", [128, DC * 3 * D], BF16),
            nc.sbuf_tensor("qkvb_cs", [128, 24], F32),
            nc.sbuf_tensor("qkvb_rs_bf", [1, 3 * D], BF16),
            nc.sbuf_tensor("gateb_cs", [128, 8], F32),
            nc.sbuf_tensor("lng_cs", [128, 8], F32),
            nc.sbuf_tensor("lnb_cs", [128, 8], F32),
            nc.sbuf_tensor("hmask_s", [128, 1], F32),
            nc.sbuf_tensor("ident_s", [128, 128], BF16),
            nc.sbuf_tensor("triu_s", [128, 128], BF16),
            nc.sbuf_tensor("onesr", [1, 128], F32),
            nc.sbuf_tensor("onesr_bf", [1, 128], BF16),
            nc.sbuf_tensor("onesc", [128, 1], F32),
            nc.sbuf_tensor("onesc_bf", [128, 1], BF16),
            nc.sbuf_tensor("cpool_s", [128, 8], F32),
            nc.sbuf_tensor("nonce_sb", [1, 4], mybir.dt.int32),
            nc.sbuf_tensor("flag_sb", [1, 4], mybir.dt.int32),
        )
        (xg, scratch, xn, q_bf, kfm, vtm, sc_sb, tmp1,
         rowbuf) = _cm(
            ctx,
            nc.sbuf_tensor("xg", [128, DC * T], F32),
            nc.sbuf_tensor("scratch", [128, DC * T], F32),
            nc.sbuf_tensor("xn", [128, DC * T], BF16),
            nc.sbuf_tensor("q_bf", [128, DC * T], BF16),
            nc.sbuf_tensor("kfm", [128, DC * T], BF16),
            nc.sbuf_tensor("vtm", [128, TC * D], BF16),
            nc.sbuf_tensor("sc_sb", [128, TC * T], BF16),
            nc.sbuf_tensor("tmp1", [128, T], F32),
            nc.sbuf_tensor("rowbuf", [1, 3 * T], F32),
        )
        xsem = ctx.enter_context(nc.semaphore("xsem"))
        gate_bf = q_bf
        pb = _cm(ctx, *[nc.psum_tensor(f"pb{i}", [128, 512], F32)
                        for i in range(8)])
        semh = _cm(ctx, *[nc.semaphore(f"sm_{n}") for n in CTRS])
        sems = dict(zip(CTRS, semh))
        block = ctx.enter_context(nc.Block())

        P = Prog()

        def ld(dst, src):
            P.emit("y", lambda y, d=dst, s=src: y.dma_start(out=d, in_=s),
                   inc="ld", inc_n=16)

        for dst, src in ((qkvb_cs[:], qkvb_c_in),
                         (gateb_cs[:], gateb_c_in), (lng_cs[:], lng_c_in),
                         (lnb_cs[:], lnb_c_in), (hmask_s[:], hmask_in),
                         (ident[:], ident_in), (triu[:], triu_in),
                         (onesr[:], onesr_in), (cpool[:], cpool_in)):
            ld(dst, src)
        ld(nonce_sb[:], nonce_in)
        for ic in range(DC):
            P.emit("g", lambda g, ic=ic: g.dma_start(
                out=qkvw[:, ic * 3 * D:(ic + 1) * 3 * D],
                in_=qkvw_in[ic * 128:(ic + 1) * 128, :]), inc="wld", inc_n=16)
        P.emit("g", lambda g: g.dma_start(out=qkvb_rs_bf[:], in_=qkvb_r_in),
               inc="wld", inc_n=16)
        P.v(lambda v: v.memset(onesc[:], 1.0))
        P.v(lambda v: v.memset(onesc_bf[:], 1.0))
        P.v(lambda v: v.memset(onesr_bf[:], 1.0))
        P.barrier()

        def ln_fm(src_sb, dst_fp32, dst_bf, gb):
            prev_pe = 0
            for icc in range(DC):
                av = P.s(lambda s, icc=icc: s.activation(
                    tmp1[:], src_sb[:, icc * T:(icc + 1) * T], AF.Square),
                    waits=[("pe", prev_pe)] if prev_pe else [])
                P.t(lambda t, icc=icc: t.matmul(
                    pb[6][0:1, 0:T], onesc[:],
                    src_sb[:, icc * T:(icc + 1) * T],
                    start=(icc == 0), stop=(icc == DC - 1)))
                prev_pe = P.t(lambda t, icc=icc: t.matmul(
                    pb[7][0:1, 0:T], onesc[:], tmp1[:],
                    start=(icc == 0), stop=(icc == DC - 1)),
                    waits=[("act", av)])
            pev = P.c["pe"]
            P.v(lambda v: v.tensor_scalar_mul(rowbuf[0:1, 0:T],
                                              pb[6][0:1, 0:T], -1.0 / D),
                waits=[("pe", pev)])
            P.v(lambda v: v.tensor_scalar_mul(rowbuf[0:1, 2 * T:3 * T],
                                              pb[7][0:1, 0:T], 1.0 / D))
            P.v(lambda v: v.tensor_tensor(rowbuf[0:1, T:2 * T],
                                          rowbuf[0:1, 0:T], rowbuf[0:1, 0:T],
                                          ALU.mult))
            dv = P.v(lambda v: v.tensor_tensor(
                rowbuf[0:1, 2 * T:3 * T], rowbuf[0:1, 2 * T:3 * T],
                rowbuf[0:1, T:2 * T], ALU.subtract))
            P.s(lambda s: s.activation(rowbuf[0:1, T:2 * T],
                                       rowbuf[0:1, 2 * T:3 * T], AF.Ln,
                                       bias=cpool[0:1, 1:2]),
                waits=[("dve", dv)])
            av = P.s(lambda s: s.activation(rowbuf[0:1, 2 * T:3 * T],
                                            rowbuf[0:1, T:2 * T], AF.Exp,
                                            scale=cpool[0:1, 0:1]))
            P.t(lambda t: t.matmul(pb[5][:, 0:T], onesr[:], rowbuf[0:1, 0:T],
                                   start=True, stop=True),
                waits=[("dve", dv)])
            pev = P.t(lambda t: t.matmul(pb[7][:, 0:T], onesr[:],
                                         rowbuf[0:1, 2 * T:3 * T], start=True,
                                         stop=True), waits=[("act", av)])
            for icc in range(DC):
                sl = slice(icc * T, (icc + 1) * T)
                w = [("pe", pev)] if icc == 0 else []
                P.v(lambda v, sl=sl: v.scalar_tensor_tensor(
                    tmp1[:], src_sb[:, sl], 1.0, pb[5][:, 0:T],
                    ALU.mult, ALU.add), waits=w)
                if gb is None:
                    dsts = [d for d in (dst_fp32, dst_bf) if d is not None]
                    P.v(lambda v, sl=sl, d0=dsts[0]: v.tensor_tensor(
                        d0[:, sl], tmp1[:], pb[7][:, 0:T], ALU.mult))
                    if len(dsts) > 1:
                        P.v(lambda v, sl=sl, d0=dsts[0], d1=dsts[1]:
                            v.tensor_copy(d1[:, sl], d0[:, sl]))
                else:
                    gcol, bcol = gb
                    P.v(lambda v, sl=sl: v.tensor_tensor(
                        tmp1[:], tmp1[:], pb[7][:, 0:T], ALU.mult))
                    dsts = [d for d in (dst_fp32, dst_bf) if d is not None]
                    P.v(lambda v, sl=sl, icc=icc, d0=dsts[0]: v.tensor_scalar(
                        d0[:, sl], tmp1[:], gcol[:, icc:icc + 1],
                        bcol[:, icc:icc + 1], ALU.mult, ALU.add))
                    if len(dsts) > 1:
                        P.v(lambda v, sl=sl, d0=dsts[0], d1=dsts[1]:
                            v.tensor_copy(d1[:, sl], d0[:, sl]))

        def proj_fm(w_sb, wrow, w_off, bias_cols, boff, dst, act_fn, src_bf):
            evac = [0, 0]
            for oc in range(DC):
                bank = oc % 2

                def mm(t, oc=oc, bank=bank):
                    r = None
                    for icc in range(DC):
                        r = t.matmul(
                            pb[bank][:, 0:T],
                            w_sb[:, icc * wrow + w_off + oc * 128:
                                 icc * wrow + w_off + (oc + 1) * 128],
                            src_bf[:, icc * T:(icc + 1) * T],
                            start=(icc == 0), stop=(icc == DC - 1))
                    return r
                pev = P.t(mm, waits=[("act", evac[bank])])
                evac[bank] = P.s(lambda s, oc=oc, bank=bank: s.activation(
                    dst[:, oc * T:(oc + 1) * T], pb[bank][:, 0:T], act_fn,
                    bias=bias_cols[:, boff + oc:boff + oc + 1]),
                    waits=[("pe", pev)])

        def elu1(src, dst, width):
            dv = P.v(lambda v: v.tensor_scalar_min(tmp1[:, 0:width], src, 0.0))
            av = P.s(lambda s: s.activation(tmp1[:, 0:width], tmp1[:, 0:width],
                                            AF.Exp), waits=[("dve", dv)])
            P.v(lambda v: v.scalar_tensor_tensor(dst, src, 0.0,
                                                 tmp1[:, 0:width], ALU.max,
                                                 ALU.add), waits=[("act", av)])

        def proj_tm(w_off3, dst_tm):
            evac = [0, 0]
            for tc in range(TC):
                for no in range(2):
                    bank = (tc * 2 + no) % 2

                    def mm(t, tc=tc, no=no, bank=bank):
                        for icc in range(DC):
                            t.matmul(
                                pb[bank][:, 0:512],
                                xn[:, icc * T + tc * 128:icc * T + (tc + 1) * 128],
                                qkvw[:, icc * 3 * D + w_off3 + no * 512:
                                     icc * 3 * D + w_off3 + (no + 1) * 512],
                                start=(icc == 0), stop=False)
                        return t.matmul(
                            pb[bank][:, 0:512], onesr_bf[0:1, 0:128],
                            qkvb_rs_bf[0:1, w_off3 + no * 512:
                                       w_off3 + (no + 1) * 512],
                            start=False, stop=True)
                    pev = P.t(mm, waits=[("dve", evac[bank])])
                    evac[bank] = P.v(
                        lambda v, tc=tc, no=no, bank=bank: v.tensor_copy(
                            dst_tm[:, tc * D + no * 512:tc * D + (no + 1) * 512],
                            pb[bank][:, 0:512]), waits=[("pe", pev)])

        with ExitStack() as p1ctx:
            (gatew, ktm, S_sb, ksum) = _cm(
                p1ctx,
                nc.sbuf_tensor("gatew", [128, DC * D], BF16),
                nc.sbuf_tensor("ktm", [128, TC * D], BF16),
                nc.sbuf_tensor("S_sb", [128, DC * D], F32),
                nc.sbuf_tensor("ksum", [128, DC], F32),
            )
            for ic in range(DC):
                P.emit("g", lambda g, ic=ic: g.dma_start(
                    out=gatew[:, ic * D:(ic + 1) * D],
                    in_=gatew_in[ic * 128:(ic + 1) * 128, :]),
                    inc="wld", inc_n=16)
            P.v(lambda v: v.memset(S_sb[:], 0.0))
            P.v(lambda v: v.memset(ksum[:], 0.0))
            P.barrier()

            for gidx in range(NG):
                def ldx(y, gidx=gidx):
                    return y.dma_start(
                        out=xg[:].rearrange("p (c t) -> p c t", t=T),
                        in_=x_in[:, gidx * T:(gidx + 1) * T].rearrange(
                            "(c p) t -> p c t", p=128))
                lv = P.emit("y", ldx, inc="ld", inc_n=16)
                wv = P.c["wld"]
                for eng in ("t", "v", "s"):
                    P.emit(eng, None, waits=[("ld", lv), ("wld", wv)])

                ln_fm(xg, None, xn, gb=(lng_cs, lnb_cs))
                P.barrier()
                P.emit("y", lambda y, gidx=gidx: y.dma_start(
                    out=bass.AP(xnd_t, gidx * 128 * DC * T,
                                [[DC * T, 128], [1, DC * T]]),
                    in_=xn[:]), inc="st", inc_n=16)

                proj_fm(gatew, D, 0, gateb_cs, 0, gate_bf, AF.Sigmoid, xn)
                proj_fm(qkvw, 3 * D, D, qkvb_cs, 8, scratch, AF.Identity, xn)
                P.barrier()
                for oc in range(DC):
                    sl = slice(oc * T, (oc + 1) * T)
                    P.v(lambda v, sl=sl: v.tensor_tensor(
                        scratch[:, sl], scratch[:, sl], gate_bf[:, sl],
                        ALU.mult))
                    elu1(scratch[:, sl], kfm[:, sl], T)
                P.barrier()
                P.emit("y", lambda y, gidx=gidx: y.dma_start(
                    out=bass.AP(kfmd_t, gidx * 128 * DC * T,
                                [[DC * T, 128], [1, DC * T]]),
                    in_=kfm[:]), inc="st", inc_n=16)
                proj_tm(2 * D, vtm)
                P.barrier()
                P.emit("y", lambda y, gidx=gidx: y.dma_start(
                    out=bass.AP(vtmd_t, gidx * 128 * TC * D,
                                [[TC * D, 128], [1, TC * D]]),
                    in_=vtm[:]), inc="st", inc_n=16)
                evac = [0, 0]
                for icc in range(DC):
                    for tc in range(TC):
                        bank = (icc * TC + tc) % 2
                        pev = P.t(lambda t, icc=icc, tc=tc, bank=bank:
                                  t.transpose(
                            pb[bank][:].bitcast(BF16)[:, 0:128],
                            kfm[:, icc * T + tc * 128:icc * T + (tc + 1) * 128],
                            ident[:]), waits=[("dve", evac[bank])])
                        evac[bank] = P.v(
                            lambda v, icc=icc, tc=tc, bank=bank: v.tensor_copy(
                                ktm[:, tc * D + icc * 128:tc * D + (icc + 1) * 128],
                                pb[bank][:].bitcast(BF16)[:, 0:128]),
                            waits=[("pe", pev)])
                P.barrier()
                evac = [0, 0]
                for dk in range(DC):
                    for dvh in range(2):
                        bank = (dk * 2 + dvh) % 2

                        def mm(t, dk=dk, dvh=dvh, bank=bank):
                            r = None
                            for tc in range(TC):
                                r = t.matmul(
                                    pb[bank][:, 0:512],
                                    ktm[:, tc * D + dk * 128:tc * D + (dk + 1) * 128],
                                    vtm[:, tc * D + dvh * 512:tc * D + (dvh + 1) * 512],
                                    start=(tc == 0), stop=(tc == TC - 1))
                            return r
                        pev = P.t(mm, waits=[("dve", evac[bank])])
                        sl = slice(dk * D + dvh * 512, dk * D + (dvh + 1) * 512)
                        evac[bank] = P.v(lambda v, sl=sl, bank=bank:
                                         v.tensor_tensor(S_sb[:, sl],
                                                         S_sb[:, sl],
                                                         pb[bank][:, 0:512],
                                                         ALU.add),
                                         waits=[("pe", pev)])
                for dk in range(DC):
                    def mm(t, dk=dk):
                        r = None
                        for tc in range(TC):
                            r = t.matmul(
                                pb[6][:, 0:1],
                                ktm[:, tc * D + dk * 128:tc * D + (dk + 1) * 128],
                                onesc_bf[:],
                                start=(tc == 0), stop=(tc == TC - 1))
                        return r
                    pev = P.t(mm, waits=[("dve", P.c["dve"])])
                    P.v(lambda v, dk=dk: v.tensor_tensor(
                        ksum[:, dk:dk + 1], ksum[:, dk:dk + 1], pb[6][:, 0:1],
                        ALU.add), waits=[("pe", pev)])
                P.barrier()
                if gidx < NG - 1:
                    P.emit("y", lambda y, gidx=gidx: y.dma_start(
                        out=bass.AP(spill_t, gidx * 128 * DC * D,
                                    [[DC * D, 128], [1, DC * D]]),
                        in_=S_sb[:]), inc="st", inc_n=16,
                        waits=[("dve", P.c["dve"])])
                    P.emit("y", lambda y, gidx=gidx: y.dma_start(
                        out=bass.AP(kspill_t, gidx * 128 * DC,
                                    [[DC, 128], [1, DC]]),
                        in_=ksum[:]), inc="st", inc_n=16)
                    P.barrier()

            if _L0UP < 2:
                P.emit("y", None, waits=[("st", P.c["st"])])
                P.finalize(nc, block, sems)
                return nc
            def xchg(g):
                with (g.register("r_slot") as r_slot,
                      g.register("r_poff") as r_poff,
                      g.register("r_tmp") as r_tmp,
                      g.register("r_iend") as r_iend,
                      g.register("r_non") as r_non,
                      g.register("r_fc") as r_fc):
                    pid = nc.partition_id()
                    g.reg_mov(r_slot, pid)
                    g.reg_mod(r_slot, r_slot, 2)
                    g.reg_mov(r_tmp, r_slot)
                    g.reg_mul(r_tmp, r_tmp, 128 * DC * D)
                    g.dma_start(out=bass.AP(shS_t, r_tmp,
                                            [[DC * D, 128], [1, DC * D]]),
                                in_=S_sb[:]).then_inc(xsem, 16)
                    g.reg_mov(r_tmp, r_slot)
                    g.reg_mul(r_tmp, r_tmp, 128 * DC)
                    g.dma_start(out=bass.AP(shK_t, r_tmp,
                                            [[DC, 128], [1, DC]]),
                                in_=ksum[:]).then_inc(xsem, 16)
                    g.wait_ge(xsem, 32)
                    g.reg_mov(r_tmp, r_slot)
                    g.reg_mul(r_tmp, r_tmp, 16)
                    g.dma_start(out=bass.AP(shF_t, r_tmp, [[16, 1], [1, 1]]),
                                in_=nonce_sb[0:1, 0:1]).then_inc(xsem, 16)
                    g.wait_ge(xsem, 48)
                    g.reg_mov(r_poff, 1)
                    g.reg_sub(r_poff, r_poff, r_slot)
                    g.reg_mul(r_poff, r_poff, 16)
                    g.reg_load(r_non, nonce_sb[0:1, 0:1])
                    g.reg_mov(r_iend, 500000)
                    g.reg_mov(r_fc, 48)
                    with g.Fori(0, r_iend):
                        g.dma_start(out=flag_sb[0:1, 0:1],
                                    in_=bass.AP(shF_t, r_poff,
                                                [[16, 1], [1, 1]])
                                    ).then_inc(xsem, 16)
                        g.reg_add(r_fc, r_fc, 16)
                        g.wait_ge(xsem, r_fc)
                        g.reg_load(r_tmp, flag_sb[0:1, 0:1])
                        g.reg_sub(r_tmp, r_tmp, r_non)
                        with g.If_eq(r_tmp, 0):
                            g.reg_mov(r_iend, 0)
                    g.reg_mov(r_tmp, 1)
                    g.reg_sub(r_tmp, r_tmp, r_slot)
                    g.reg_mul(r_tmp, r_tmp, 128 * DC * D)
                    g.dma_start(out=S_sb[:],
                                in_=bass.AP(shS_t, r_tmp,
                                            [[DC * D, 128], [1, DC * D]])
                                ).then_inc(xsem, 16)
                    g.reg_mov(r_tmp, 1)
                    g.reg_sub(r_tmp, r_tmp, r_slot)
                    g.reg_mul(r_tmp, r_tmp, 128 * DC)
                    g.dma_start(out=ksum[:],
                                in_=bass.AP(shK_t, r_tmp,
                                            [[DC, 128], [1, DC]])
                                ).then_inc(xsem, 16)
                    g.reg_add(r_fc, r_fc, 32)
                    g.wait_ge(xsem, r_fc)
                    return g.memset(flag_sb[0:1, 1:2], 0)
            import os as _os
            if _os.environ.get("L0_NOX") == "1":
                def xchg_stub(g):
                    return g.memset(flag_sb[0:1, 1:2], 0)
                xv = P.emit("g", xchg_stub, inc="pool", inc_n=1)
            else:
                xv = P.emit("g", xchg, inc="pool", inc_n=1)
            for dk in range(DC):
                P.v(lambda v, dk=dk: v.tensor_scalar_mul(
                    S_sb[:, dk * D:(dk + 1) * D],
                    S_sb[:, dk * D:(dk + 1) * D], hmask_s[:, 0:1]),
                    waits=[("pool", xv)] if dk == 0 else [])
            dv = P.v(lambda v: v.tensor_scalar_mul(ksum[:], ksum[:],
                                                   hmask_s[:, 0:1]))
            P.emit("y", lambda y: y.dma_start(
                out=bass.AP(smask_t, 0, [[DC * D, 128], [1, DC * D]]),
                in_=S_sb[:]), inc="st", inc_n=16, waits=[("dve", dv)])
            P.emit("y", lambda y: y.dma_start(
                out=bass.AP(kmask_t, 0, [[DC, 128], [1, DC]]),
                in_=ksum[:]), inc="st", inc_n=16)
            P.barrier()

        if _L0UP < 3:
            P.emit("y", None, waits=[("st", P.c["st"])])
            P.finalize(nc, block, sems)
            return nc
        with ExitStack() as p2ctx:
            (w0s, swork, S_stage, kwork, kwork_bf, sadd, kadd) = _cm(
                p2ctx,
                nc.sbuf_tensor("w0s", [128, DC * D], BF16),
                nc.sbuf_tensor("swork", [128, DC * D], F32),
                nc.sbuf_tensor("S_stage", [128, 2 * DC * 128], BF16),
                nc.sbuf_tensor("kwork", [128, DC], F32),
                nc.sbuf_tensor("kwork_bf", [128, DC], BF16),
                nc.sbuf_tensor("sadd", [128, 4 * D], F32),
                nc.sbuf_tensor("kadd", [128, DC], F32),
            )
            for ic in range(DC):
                lv = P.emit("y", lambda y, ic=ic: y.dma_start(
                    out=scratch[:, 0:D],
                    in_=w0t_in[ic * 128:(ic + 1) * 128, :]),
                    inc="ld", inc_n=16, waits=[("act", P.c["act"])])
                P.s(lambda s, ic=ic: s.activation(
                    w0s[:, ic * D:(ic + 1) * D], scratch[:, 0:D], AF.Sign),
                    waits=[("ld", lv)])
            P.barrier()

            if _L0UP < 31:
                P.emit("y", None, waits=[("st", P.c["st"]),
                                         ("act", P.c["act"])])
                P.finalize(nc, block, sems)
                return nc
            for gidx in range(NG):
                lv = P.emit("y", lambda y, gidx=gidx: y.dma_start(
                    out=xg[:].rearrange("p (c t) -> p c t", t=T),
                    in_=x_in[:, gidx * T:(gidx + 1) * T].rearrange(
                        "(c p) t -> p c t", p=128)), inc="ld", inc_n=16)
                lv = P.emit("y", lambda y, gidx=gidx: y.dma_start(
                    out=xn[:],
                    in_=bass.AP(xnd_t, gidx * 128 * DC * T,
                                [[DC * T, 128], [1, DC * T]])),
                    inc="ld", inc_n=16)
                lv = P.emit("y", lambda y, gidx=gidx: y.dma_start(
                    out=kfm[:],
                    in_=bass.AP(kfmd_t, gidx * 128 * DC * T,
                                [[DC * T, 128], [1, DC * T]])),
                    inc="ld", inc_n=16)
                lv = P.emit("y", lambda y, gidx=gidx: y.dma_start(
                    out=vtm[:],
                    in_=bass.AP(vtmd_t, gidx * 128 * TC * D,
                                [[TC * D, 128], [1, TC * D]])),
                    inc="ld", inc_n=16)
                if gidx == 0:
                    sv = P.emit("y", lambda y: y.dma_start(
                        out=swork[:],
                        in_=bass.AP(smask_t, 0,
                                    [[DC * D, 128], [1, DC * D]])),
                        inc="ld", inc_n=16,
                        waits=[("pe", P.c["pe"]), ("dve", P.c["dve"])])
                    sv = P.emit("y", lambda y: y.dma_start(
                        out=kwork[:],
                        in_=bass.AP(kmask_t, 0, [[DC, 128], [1, DC]])),
                        inc="ld", inc_n=16)
                else:
                    sv = P.emit("y", lambda y, gidx=gidx: y.dma_start(
                        out=swork[:],
                        in_=bass.AP(spill_t, (gidx - 1) * 128 * DC * D,
                                    [[DC * D, 128], [1, DC * D]])),
                        inc="ld", inc_n=16,
                        waits=[("pe", P.c["pe"]), ("dve", P.c["dve"])])
                    sv2 = P.emit("y", lambda y, gidx=gidx: y.dma_start(
                        out=kwork[:],
                        in_=bass.AP(kspill_t, (gidx - 1) * 128 * DC,
                                    [[DC, 128], [1, DC]])),
                        inc="ld", inc_n=16)
                    for hh in range(2):
                        av2 = P.emit("y", lambda y, hh=hh: y.dma_start(
                            out=sadd[:],
                            in_=bass.AP(smask_t, hh * 4 * D,
                                        [[DC * D, 128], [1, 4 * D]])),
                            inc="ld", inc_n=16,
                            waits=[("dve", P.c["dve"])] if hh else
                            [("ld", sv2)])
                        for qq in range(4):
                            P.v(lambda v, hh=hh, qq=qq: v.tensor_tensor(
                                swork[:, hh * 4 * D + qq * D:
                                      hh * 4 * D + (qq + 1) * D],
                                swork[:, hh * 4 * D + qq * D:
                                      hh * 4 * D + (qq + 1) * D],
                                sadd[:, qq * D:(qq + 1) * D], ALU.add),
                                waits=[("ld", av2)] if qq == 0 else [])
                    av2 = P.emit("y", lambda y: y.dma_start(
                        out=kadd[:],
                        in_=bass.AP(kmask_t, 0, [[DC, 128], [1, DC]])),
                        inc="ld", inc_n=16)
                lds = P.c["ld"]
                wlds = P.c["wld"]
                for eng in ("t", "v", "s"):
                    P.emit(eng, None, waits=[("ld", lds), ("wld", wlds)])
                if gidx == 0:
                    P.v(lambda v: v.tensor_copy(kwork_bf[:], kwork[:]))
                else:
                    P.v(lambda v: v.tensor_tensor(kwork_bf[:], kwork[:],
                                                  kadd[:], ALU.add))
                if _os0.environ.get("L0_DBG") == "1" and gidx == 1:
                    P.emit("y", lambda y: y.dma_start(
                        out=x2_out[:, 0:TOK].rearrange(
                            "(c p) t -> p c t", p=128)[:, 0:2, :],
                        in_=swork[:, 0:2 * TOK].rearrange(
                            "p (c t) -> p c t", t=TOK)),
                        inc="st", inc_n=16, waits=[("dve", P.c["dve"])])
                    P.emit("y", lambda y: y.dma_start(
                        out=x2_out[256:384, 0:DC], in_=kwork[:]),
                        inc="st", inc_n=16)
                    P.emit("y", lambda y: y.dma_start(
                        out=x2_out[384:512, 0:DC], in_=kadd[:]),
                        inc="st", inc_n=16)
                    P.barrier()

                if _L0UP < 320:
                    P.barrier()
                    continue
                proj_fm(qkvw, 3 * D, 0, qkvb_cs, 0, scratch, AF.Identity, xn)
                P.barrier()
                for oc in range(DC):
                    sl = slice(oc * T, (oc + 1) * T)
                    elu1(scratch[:, sl], q_bf[:, sl], T)
                P.barrier()

                if _L0UP < 330:
                    P.barrier()
                    continue
                for j in range(TC):
                    span = (TC - j) * 128
                    bank = j % 2

                    def mm(t, j=j, span=span, bank=bank):
                        r = None
                        for icc in range(DC):
                            r = t.matmul(
                                pb[bank][:, 0:span],
                                kfm[:, icc * T + j * 128:icc * T + (j + 1) * 128],
                                q_bf[:, icc * T + j * 128:icc * T + j * 128 + span],
                                start=(icc == 0), stop=(icc == DC - 1))
                        return r
                    pev = P.t(mm)
                    P.v(lambda v, j=j, bank=bank: v.tensor_tensor(
                        sc_sb[:, j * T:j * T + 128], pb[bank][:, 0:128],
                        triu[:], ALU.mult), waits=[("pe", pev)])
                    if span > 128:
                        P.v(lambda v, j=j, span=span, bank=bank: v.tensor_copy(
                            sc_sb[:, j * T + 128:j * T + span],
                            pb[bank][:, 128:span]))
                P.barrier()

                for j in range(TC):
                    span = (TC - j) * 128
                    P.t(lambda t, j=j, span=span: t.matmul(
                        pb[2][0:1, j * 128:512], onesc_bf[:],
                        sc_sb[:, j * T:j * T + span],
                        start=(j == 0), stop=False))
                for dk in range(DC):
                    P.t(lambda t, dk=dk: t.matmul(
                        pb[2][0:1, 0:T], kwork_bf[:, dk:dk + 1],
                        q_bf[:, dk * T:(dk + 1) * T],
                        start=False, stop=(dk == DC - 1)))
                pev = P.c["pe"]
                dv = P.v(lambda v: v.tensor_scalar_add(
                    rowbuf[0:1, 0:T], pb[2][0:1, 0:T], DEN_EPS),
                    waits=[("pe", pev)])
                dv = P.v(lambda v: v.reciprocal(rowbuf[0:1, T:2 * T],
                                                rowbuf[0:1, 0:T]))
                pev = P.t(lambda t: t.matmul(pb[3][:, 0:T], onesr[:],
                                             rowbuf[0:1, T:2 * T],
                                             start=True, stop=True),
                          waits=[("dve", dv)])
                P.v(lambda v: v.tensor_copy(scratch[:, 0:T], pb[3][:, 0:T]),
                    waits=[("pe", pev)])
                P.barrier()

                if _os0.environ.get("L0_DBG") == "2" and gidx == 1:
                    P.v(lambda v: v.tensor_copy(kadd[:], kwork_bf[:]))
                    P.emit("y", lambda y: y.dma_start(
                        out=x2_out[640:768, 0:DC], in_=kadd[:]),
                        inc="st", inc_n=16, waits=[("dve", P.c["dve"])])
                    P.emit("y", lambda y: y.dma_start(
                        out=x2_out[0:1, 0:T], in_=rowbuf[0:1, T:2 * T]),
                        inc="st", inc_n=16)
                    P.emit("y", lambda y: y.dma_start(
                        out=x2_out[768:896, 0:DC], in_=kwork[:]),
                        inc="st", inc_n=16)
                    P.barrier()
                if _L0UP < 340:
                    P.barrier()
                    continue
                evac = [0, 0]
                pe_hist = [0, 0]
                for dvc in range(DC):
                    bank = 4 + dvc % 2
                    stg = dvc % 2
                    last = 0
                    for dk in range(DC):
                        w = [("pe", pe_hist[stg])] if dk == 0 else []
                        last = P.v(lambda v, dk=dk, dvc=dvc, stg=stg:
                                   v.tensor_copy(
                            S_stage[:, stg * D + dk * 128:stg * D + (dk + 1) * 128],
                            swork[:, dk * D + dvc * 128:dk * D + (dvc + 1) * 128]),
                            waits=w)

                    def mm(t, dvc=dvc, bank=bank, stg=stg):
                        for dk in range(DC):
                            t.matmul(pb[bank][:, 0:T],
                                     S_stage[:, stg * D + dk * 128:stg * D + (dk + 1) * 128],
                                     q_bf[:, dk * T:(dk + 1) * T],
                                     start=(dk == 0), stop=False)
                        r = None
                        for j in range(TC):
                            span = (TC - j) * 128
                            r = t.matmul(
                                pb[bank][:, j * 128:512],
                                vtm[:, j * D + dvc * 128:j * D + (dvc + 1) * 128],
                                sc_sb[:, j * T:j * T + span],
                                start=False, stop=(j == TC - 1))
                        return r
                    pev = P.t(mm, waits=[("dve", last),
                                         ("dve", evac[dvc % 2])])
                    pe_hist[stg] = pev
                    P.v(lambda v, dvc=dvc, bank=bank: v.tensor_tensor(
                        tmp1[:], pb[bank][:, 0:T], scratch[:, 0:T], ALU.mult),
                        waits=[("pe", pev)])
                    evac[dvc % 2] = P.v(
                        lambda v, dvc=dvc: v.tensor_tensor(
                            xg[:, dvc * T:(dvc + 1) * T],
                            xg[:, dvc * T:(dvc + 1) * T], tmp1[:], ALU.add))
                P.barrier()

                if _L0UP < 350:
                    P.barrier()
                    continue
                ln_fm(xg, scratch, xn, gb=None)
                P.barrier()
                evac = [0, 0]
                for oc in range(DC):
                    bank = oc % 2

                    def mm(t, oc=oc, bank=bank):
                        r = None
                        for icc in range(DC):
                            r = t.matmul(
                                pb[bank][:, 0:T],
                                w0s[:, icc * D + oc * 128:icc * D + (oc + 1) * 128],
                                xn[:, icc * T:(icc + 1) * T],
                                start=(icc == 0), stop=(icc == DC - 1))
                        return r
                    pev = P.t(mm, waits=[("dve", evac[bank])])
                    evac[bank] = P.v(
                        lambda v, oc=oc, bank=bank: v.tensor_tensor(
                            xg[:, oc * T:(oc + 1) * T],
                            scratch[:, oc * T:(oc + 1) * T],
                            pb[bank][:, 0:T], ALU.add), waits=[("pe", pev)])
                P.barrier()
                ln_fm(xg, scratch, None, gb=None)
                P.barrier()
                def st(y, gidx=gidx):
                    return y.dma_start(
                        out=x2_out[:, gidx * T:(gidx + 1) * T].rearrange(
                            "(c p) t -> p c t", p=128),
                        in_=scratch[:].rearrange("p (c t) -> p c t", t=T))
                P.emit("y", st, inc="st", inc_n=16,
                       waits=[("dve", P.c["dve"])])
                P.barrier()
                for eng in ("t", "v", "s"):
                    P.emit(eng, None, waits=[("st", P.c["st"])])

        P.emit("y", None, waits=[("st", P.c["st"])])

        P.finalize(nc, block, sems)
    return nc



def _cols(vec, p=128):
    return np.ascontiguousarray(vec.reshape(-1, p).T)


_L0_CACHE = {}


def _l0_inmaps(inp):
    x = np.asarray(inp["L0_attn_x"], dtype=np.float32)
    qkv_w = np.asarray(inp["L0_attn_qkv_w"], np.float32)
    qkv_b = np.asarray(inp["L0_attn_qkv_b"], np.float32)
    gate_w = np.asarray(inp["L0_attn_gate_w"], np.float32)
    gate_b = np.asarray(inp["L0_attn_gate_b"], np.float32)
    ln_g = np.asarray(inp["L0_attn_ln_g"], np.float32)
    ln_b = np.asarray(inp["L0_attn_ln_b"], np.float32)
    w0 = np.asarray(inp["L0_ffn_weight"], np.float32)
    w0t = np.ascontiguousarray(w0.T)

    ident16 = np.eye(128, dtype=BF)
    triu = np.triu(np.ones((128, 128))).astype(BF)
    onesr = np.ones((1, 128), np.float32)
    cpool = np.zeros((128, 8), np.float32)
    cpool[:, 0] = -0.5
    cpool[:, 1] = LN_EPS
    cpool[:, 2] = DEN_EPS
    cpool[:, 3] = 1.0

    nonce = np.full((1, 4), np.random.randint(1, 2 ** 30), np.int32)
    maps = []
    for c in range(8):
        b, h = c // 2, c % 2
        xs = np.ascontiguousarray(x[b, h * TOK:(h + 1) * TOK, :].T)
        hm = np.full((128, 1), float(h), np.float32)
        maps.append({
            "x_fm": xs, "nonce": nonce, "qkv_w": qkv_w, "gate_w": gate_w,
            "w0t": w0t, "qkv_b_c": _cols(qkv_b),
            "qkv_b_r": qkv_b.reshape(1, -1), "gate_b_c": _cols(gate_b),
            "ln_g_c": _cols(ln_g), "ln_b_c": _cols(ln_b), "hmask": hm,
            "ident16": ident16, "triu": triu, "ones_r": onesr,
            "cpool": cpool,
        })
    return maps


def run_l0(inp):
    if "l0" not in _L0_CACHE:
        _L0_CACHE["l0"] = build_l0()
    nc = _L0_CACHE["l0"]
    res = run_bass_kernel_spmd(nc, _l0_inmaps(inp), core_ids=list(range(8)))
    return [r["x2"] for r in res.results]


def build_l1():
    import os
    _UPTO = int(os.environ.get("L1_UPTO", "9999"))
    nc = bass.Bass("TRN2", target_bir_lowering=False)

    y0_in = nc.dram_tensor("y0_fm", [D, TOK], F32, kind="ExternalInput").ap()
    qw_in = nc.dram_tensor("qw", [D, D], F32, kind="ExternalInput").ap()
    kw_in = nc.dram_tensor("kw", [D, D], F32, kind="ExternalInput").ap()
    vw_in = nc.dram_tensor("vw", [D, D], F32, kind="ExternalInput").ap()
    ow_in = nc.dram_tensor("ow", [D, D], F32, kind="ExternalInput").ap()
    qb_in = nc.dram_tensor("qb_c", [128, 8], F32, kind="ExternalInput").ap()
    kb_in = nc.dram_tensor("kb_c", [128, 8], F32, kind="ExternalInput").ap()
    vb_in = nc.dram_tensor("vb_c", [128, 8], F32, kind="ExternalInput").ap()
    ob_in = nc.dram_tensor("ob_c", [128, 8], F32, kind="ExternalInput").ap()
    bk_in = nc.dram_tensor("bk_c", [128, 8], F32, kind="ExternalInput").ap()
    bv_in = nc.dram_tensor("bv_c", [128, 8], F32, kind="ExternalInput").ap()
    rel_in = nc.dram_tensor("rel_emb", [32, 16], F32, kind="ExternalInput").ap()
    grw_in = nc.dram_tensor("grep_w", [64, 8], F32, kind="ExternalInput").ap()
    grb_in = nc.dram_tensor("grep_b", [1, 8], F32, kind="ExternalInput").ap()
    gra_in = nc.dram_tensor("grep_a", [16, 1], F32, kind="ExternalInput").ap()
    onehot_in = nc.dram_tensor("onehot", [32, 20], F32, kind="ExternalInput").ap()
    P2_in = nc.dram_tensor("P2", [64, 128], F32, kind="ExternalInput").ap()
    Eb_in = nc.dram_tensor("E_big", [128, 16 * DC], BF16, kind="ExternalInput").ap()
    selRed_in = nc.dram_tensor("selRed", [128, 256], BF16, kind="ExternalInput").ap()
    selB_in = nc.dram_tensor("selB", [16, 1024], BF16, kind="ExternalInput").ap()
    selG_in = nc.dram_tensor("selG", [16, 256], F32, kind="ExternalInput").ap()
    w1t_in = nc.dram_tensor("w1t", [32 * 128, D], F32, kind="ExternalInput").ap()
    w2t_in = nc.dram_tensor("w2t", [32 * 128, D], F32, kind="ExternalInput").ap()
    h1t_in = nc.dram_tensor("h1t", [32 * 128, 4096], F32, kind="ExternalInput").ap()
    h2t_in = nc.dram_tensor("h2t", [32 * 128, 4096], F32, kind="ExternalInput").ap()
    w3t_in = nc.dram_tensor("w3t", [8 * 128, 4096], F32, kind="ExternalInput").ap()
    w1s_in = nc.dram_tensor("w1s_c", [128, 32], F32, kind="ExternalInput").ap()
    w2s_in = nc.dram_tensor("w2s_c", [128, 32], F32, kind="ExternalInput").ap()
    h1s_in = nc.dram_tensor("h1s_c", [128, 32], F32, kind="ExternalInput").ap()
    h2s_in = nc.dram_tensor("h2s_c", [128, 32], F32, kind="ExternalInput").ap()
    w3s_in = nc.dram_tensor("w3s_c", [128, 8], F32, kind="ExternalInput").ap()
    onesr_in = nc.dram_tensor("ones_r", [1, 128], F32, kind="ExternalInput").ap()
    cpool_in = nc.dram_tensor("cpool", [128, 8], F32, kind="ExternalInput").ap()

    out_dram = nc.dram_tensor("out_fm", [D, TOK], F32, kind="ExternalOutput").ap()
    y1_dbg = nc.dram_tensor("y1_dbg", [D, TOK], F32, kind="ExternalOutput").ap()
    y1_dram = nc.dram_tensor("y1_scr", [D, TOK], F32).ap()

    TA = 1024
    PH = 256

    from contextlib import ExitStack
    ctx = ExitStack()
    with ctx:
        (qbc, kbc, vbc, obc, bkc, bvc, bkc_bf, bvc_bf, rel_s, grw_s, grb_s,
         gra_s, onehot_s, P2_s, Eb_s, selRed_s, selB_s, selG_s, bkc_sw,
         bvc_sw, onesr, onesc, cpool, gwcol,
         Gmat, pb_sb, sgb, w1s, w2s, h1s, h2s, w3s, rowbuf, tmp1) = _cm(
            ctx,
            nc.sbuf_tensor("qbc", [128, 8], F32),
            nc.sbuf_tensor("kbc", [128, 8], F32),
            nc.sbuf_tensor("vbc", [128, 8], F32),
            nc.sbuf_tensor("obc", [128, 8], F32),
            nc.sbuf_tensor("bkc", [128, 8], F32),
            nc.sbuf_tensor("bvc", [128, 8], F32),
            nc.sbuf_tensor("bkc_bf", [128, 8], BF16),
            nc.sbuf_tensor("bvc_bf", [128, 8], BF16),
            nc.sbuf_tensor("rel_s", [32, 16], F32),
            nc.sbuf_tensor("grw_s", [64, 8], F32),
            nc.sbuf_tensor("grb_s", [1, 8], F32),
            nc.sbuf_tensor("gra_s", [16, 1], F32),
            nc.sbuf_tensor("onehot_s", [32, 20], F32),
            nc.sbuf_tensor("P2_s", [64, 128], F32),
            nc.sbuf_tensor("Eb_s", [128, 16 * DC], BF16),
            nc.sbuf_tensor("selRed_s", [128, 256], BF16),
            nc.sbuf_tensor("selB_s", [16, 1024], BF16),
            nc.sbuf_tensor("selG_s", [16, 256], F32),
            nc.sbuf_tensor("bkc_sw", [128, 8], BF16),
            nc.sbuf_tensor("bvc_sw", [128, 8], BF16),
            nc.sbuf_tensor("onesr_s", [1, 128], F32),
            nc.sbuf_tensor("onesc_s", [128, 1], F32),
            nc.sbuf_tensor("cpool_s", [128, 8], F32),
            nc.sbuf_tensor("gwcol", [128, 1], F32),
            nc.sbuf_tensor("Gmat", [128, 16 * DC], BF16),
            nc.sbuf_tensor("pb_sb", [16, 20], F32),
            nc.sbuf_tensor("sgb", [16, 1], F32),
            nc.sbuf_tensor("w1s_s", [128, 32], F32),
            nc.sbuf_tensor("w2s_s", [128, 32], F32),
            nc.sbuf_tensor("h1s_s", [128, 32], F32),
            nc.sbuf_tensor("h2s_s", [128, 32], F32),
            nc.sbuf_tensor("w3s_s", [128, 8], F32),
            nc.sbuf_tensor("rowbuf_1", [1, 3 * T], F32),
            nc.sbuf_tensor("tmp1_1", [128, T], F32),
        )
        pb = _cm(ctx, *[nc.psum_tensor(f"pb{i}", [128, 512], F32)
                        for i in range(8)])
        semh = _cm(ctx, *[nc.semaphore(f"sm_{n}") for n in CTRS])
        sems = dict(zip(CTRS, semh))
        block = ctx.enter_context(nc.Block())
        P = Prog()

        def ld(dst, src):
            P.emit("y", lambda y, d=dst, s=src: y.dma_start(out=d, in_=s),
                   inc="ld", inc_n=16)

        for dst, srcm in ((qbc[:], qb_in), (kbc[:], kb_in), (vbc[:], vb_in),
                          (obc[:], ob_in), (bkc[:], bk_in), (bvc[:], bv_in),
                          (rel_s[:], rel_in), (grw_s[:], grw_in),
                          (grb_s[:], grb_in), (gra_s[:], gra_in),
                          (onehot_s[:], onehot_in), (P2_s[:], P2_in),
                          (onesr[:], onesr_in), (cpool[:], cpool_in),
                          (w1s[:], w1s_in), (w2s[:], w2s_in),
                          (h1s[:], h1s_in), (h2s[:], h2s_in),
                          (w3s[:], w3s_in)):
            ld(dst, srcm)
        P.emit("g", lambda g: g.dma_start(out=Eb_s[:], in_=Eb_in),
               inc="wld", inc_n=16)
        P.emit("g", lambda g: g.dma_start(out=selRed_s[:], in_=selRed_in),
               inc="wld", inc_n=16)
        P.emit("g", lambda g: g.dma_start(out=selB_s[:], in_=selB_in),
               inc="wld", inc_n=16)
        ldv, wldv = P.c["ld"], P.c["wld"]
        for eng in ("t", "v", "s"):
            P.emit(eng, None, waits=[("ld", ldv), ("wld", wldv)])
        ld(selG_s[:], selG_in)
        P.v(lambda v: v.memset(onesc[:], 1.0))
        P.v(lambda v: v.tensor_copy(bkc_bf[:], bkc[:]))
        P.v(lambda v: v.tensor_copy(bvc_bf[:], bvc[:]))
        dvb = P.c["dve"]
        P.emit("y", lambda y: y.dma_start(out=bkc_sw[0:64, :],
                                          in_=bkc_bf[64:128, :]),
               inc="ld", inc_n=16, waits=[("dve", dvb)])
        P.emit("y", lambda y: y.dma_start(out=bkc_sw[64:128, :],
                                          in_=bkc_bf[0:64, :]),
               inc="ld", inc_n=16)
        P.emit("y", lambda y: y.dma_start(out=bvc_sw[0:64, :],
                                          in_=bvc_bf[64:128, :]),
               inc="ld", inc_n=16)
        P.emit("y", lambda y: y.dma_start(out=bvc_sw[64:128, :],
                                          in_=bvc_bf[0:64, :]),
               inc="ld", inc_n=16)
        dv = P.v(lambda v: v.tensor_reduce(gwcol[0:64, 0:1], grw_s[:],
                                           AX.X, ALU.add))
        pev = P.t(lambda t: t.matmul(pb[0][:, 0:1], P2_s[:], gwcol[0:64, 0:1],
                                     start=True, stop=True),
                  waits=[("dve", dv)])
        dv = P.v(lambda v: v.tensor_copy(gwcol[:], pb[0][:, 0:1]),
                 waits=[("pe", pev)])
        for c in range(DC):
            P.v(lambda v, c=c: v.tensor_scalar_mul(
                Gmat[:, c * 16:(c + 1) * 16], Eb_s[:, c * 16:(c + 1) * 16],
                gwcol[:, 0:1]))
        pev = P.t(lambda t: t.matmul(pb[1][0:16, 0:20], rel_s[:], onehot_s[:],
                                     start=True, stop=True))
        P.v(lambda v: v.tensor_copy(pb_sb[:], pb[1][0:16, 0:20]),
            waits=[("pe", pev)])
        dv = P.v(lambda v: v.tensor_reduce(rowbuf[0:1, 0:1], grb_s[:], AX.X,
                                           ALU.add))
        pev = P.t(lambda t: t.matmul(pb[2][0:16, 0:1], onesr[0:1, 0:16],
                                     rowbuf[0:1, 0:1], start=True, stop=True),
                  waits=[("dve", dv)])
        P.v(lambda v: v.tensor_copy(sgb[:], pb[2][0:16, 0:1]),
            waits=[("pe", pev)])
        P.barrier()

        def ln_fm_at(src_sb, col0, dst_sb, dcol0, T_=T):
            sw = src_sb.shape[1] // DC
            dw = dst_sb.shape[1] // DC
            prev_pe = 0
            for icc in range(DC):
                sl = slice(col0 + icc * sw, col0 + icc * sw + T_)
                av = P.s(lambda s, sl=sl: s.activation(
                    tmp1[:, 0:T_], src_sb[:, sl], AF.Square),
                    waits=[("pe", prev_pe)] if prev_pe else [])
                P.t(lambda t, sl=sl, icc=icc: t.matmul(
                    pb[6][0:1, 0:T_], onesc[:], src_sb[:, sl],
                    start=(icc == 0), stop=(icc == DC - 1)))
                prev_pe = P.t(lambda t, icc=icc: t.matmul(
                    pb[7][0:1, 0:T_], onesc[:], tmp1[:, 0:T_],
                    start=(icc == 0), stop=(icc == DC - 1)),
                    waits=[("act", av)])
            P.v(lambda v: v.tensor_scalar_mul(rowbuf[0:1, 0:T_],
                                              pb[6][0:1, 0:T_], -1.0 / D),
                waits=[("pe", prev_pe)])
            P.v(lambda v: v.tensor_scalar_mul(rowbuf[0:1, 2 * T_:3 * T_],
                                              pb[7][0:1, 0:T_], 1.0 / D))
            P.v(lambda v: v.tensor_tensor(rowbuf[0:1, T_:2 * T_],
                                          rowbuf[0:1, 0:T_],
                                          rowbuf[0:1, 0:T_], ALU.mult))
            dv = P.v(lambda v: v.tensor_tensor(
                rowbuf[0:1, 2 * T_:3 * T_], rowbuf[0:1, 2 * T_:3 * T_],
                rowbuf[0:1, T_:2 * T_], ALU.subtract))
            P.s(lambda s: s.activation(rowbuf[0:1, T_:2 * T_],
                                       rowbuf[0:1, 2 * T_:3 * T_], AF.Ln,
                                       bias=cpool[0:1, 1:2]),
                waits=[("dve", dv)])
            av = P.s(lambda s: s.activation(rowbuf[0:1, 2 * T_:3 * T_],
                                            rowbuf[0:1, T_:2 * T_], AF.Exp,
                                            scale=cpool[0:1, 0:1]))
            P.t(lambda t: t.matmul(pb[5][:, 0:T_], onesr[0:1, 0:128],
                                   rowbuf[0:1, 0:T_], start=True, stop=True),
                waits=[("dve", dv)])
            pev = P.t(lambda t: t.matmul(pb[7][:, 0:T_], onesr[0:1, 0:128],
                                         rowbuf[0:1, 2 * T_:3 * T_],
                                         start=True, stop=True),
                      waits=[("act", av)])
            for icc in range(DC):
                ssl = slice(col0 + icc * sw, col0 + icc * sw + T_)
                dsl = slice(dcol0 + icc * dw, dcol0 + icc * dw + T_)
                w = [("pe", pev)] if icc == 0 else []
                P.v(lambda v, ssl=ssl: v.scalar_tensor_tensor(
                    tmp1[:, 0:T_], src_sb[:, ssl], 1.0, pb[5][:, 0:T_],
                    ALU.mult, ALU.add), waits=w)
                P.v(lambda v, dsl=dsl: v.tensor_tensor(
                    dst_sb[:, dsl], tmp1[:, 0:T_], pb[7][:, 0:T_], ALU.mult))

        with ExitStack() as actx:
            (wslot, x2h, x2bh, q_sb, k_sb, k_swap, acc,
             prod, pbc, gate_sb, gate_r, sc0, gp, probs, y1g, x2t) = _cm(
                actx,
                nc.sbuf_tensor("wslot", [128, DC * D], BF16),
                nc.sbuf_tensor("x2h", [128, DC * TA], F32),
                nc.sbuf_tensor("x2bh", [128, DC * TA], BF16),
                nc.sbuf_tensor("q_sb", [128, DC * TA], BF16),
                nc.sbuf_tensor("k_sb", [128, DC * TA], BF16),
                nc.sbuf_tensor("k_swap", [128, DC * TA], BF16),
                nc.sbuf_tensor("acc", [128, DC * TA], BF16),
                nc.sbuf_tensor("prod", [128, 4 * PH], BF16),
                nc.sbuf_tensor("pbc", [128, 4 * PH], BF16),
                nc.sbuf_tensor("gate_sb", [16, TA], F32),
                nc.sbuf_tensor("gate_r", [16, TA], F32),
                nc.sbuf_tensor("sc0", [16, SRC * TA], F32),
                nc.sbuf_tensor("gp", [16, SRC * PH], F32),
                nc.sbuf_tensor("probs", [16, SRC * TA], BF16),
                nc.sbuf_tensor("y1g", [128, DC * T], F32),
                nc.sbuf_tensor("x2t", [128, T], F32),
            )

            def load_w(wsrc):
                lv = P.emit("g", lambda g, s=wsrc: g.dma_start(
                    out=wslot[:].rearrange("p (c n) -> p c n", c=DC),
                    in_=s.rearrange("(c p) n -> p c n", p=128)),
                    inc="wld", inc_n=16, waits=[("pe", P.c["pe"])])
                for eng in ("t",):
                    P.emit(eng, None, waits=[("wld", lv)])

            for ph in range(2 if _UPTO >= 500 else 1):
                for t_ in range(BATCH):
                    def ldx(y, ph=ph, t_=t_):
                        return y.dma_start(
                            out=bass.AP(x2h, t_ * PH,
                                        [[DC * TA, 128], [TA, DC], [1, PH]]),
                            in_=bass.AP(y0_in.tensor, t_ * T + ph * PH,
                                        [[TOK, 128], [128 * TOK, DC],
                                         [1, PH]]))
                    lv = P.emit("y", ldx, inc="ld", inc_n=16)
                for eng in ("t", "v", "s"):
                    P.emit(eng, None, waits=[("ld", lv)])
                for icc in range(DC):
                    P.v(lambda v, icc=icc: v.tensor_copy(
                        x2bh[:, icc * TA:(icc + 1) * TA],
                        x2h[:, icc * TA:(icc + 1) * TA]))
                P.barrier()

                def proj(bias, dst):
                    evac = [0, 0]
                    for oc in range(DC):
                        bank = oc % 2

                        def mm(t, oc=oc, bank=bank):
                            r = None
                            for icc in range(DC):
                                for tg in range(2):
                                    r = t.matmul(
                                        pb[bank + 2 * tg][:, 0:T],
                                        wslot[:, icc * D + oc * 128:
                                              icc * D + (oc + 1) * 128],
                                        x2bh[:, icc * TA + tg * T:
                                             icc * TA + (tg + 1) * T],
                                        start=(icc == 0), stop=(icc == DC - 1))
                            return r
                        pev = P.t(mm, waits=[("act", evac[bank])])
                        for tg in range(2):
                            evac[bank] = P.s(
                                lambda s, oc=oc, bank=bank, tg=tg, dst=dst,
                                bias=bias: s.activation(
                                    dst[:, oc * TA + tg * T:
                                        oc * TA + (tg + 1) * T],
                                    pb[bank + 2 * tg][:, 0:T], AF.Identity,
                                    bias=bias[:, oc:oc + 1]),
                                waits=[("pe", pev)])
                load_w(qw_in)
                proj(qbc, q_sb)
                load_w(kw_in)
                proj(kbc, k_sb)
                P.barrier()

                if _UPTO < 2:
                    continue
                def gmm(t):
                    r = None
                    for icc in range(DC):
                        for tg in range(2):
                            r = t.matmul(
                                pb[4 + tg][0:16, 0:T],
                                Gmat[:, icc * 16:(icc + 1) * 16],
                                q_sb[:, icc * TA + tg * T:icc * TA + (tg + 1) * T],
                                start=(icc == 0), stop=(icc == DC - 1))
                    return r
                pev = P.t(gmm)
                for tg in range(2):
                    av = P.s(lambda s, tg=tg: s.activation(
                        gate_sb[:, tg * T:(tg + 1) * T], pb[4 + tg][0:16, 0:T],
                        AF.Sigmoid, bias=sgb[:, 0:1]), waits=[("pe", pev)])
                for t2 in range(4):
                    def pmm(t, t2=t2):
                        r = None
                        for tt in range(4):
                            r = t.matmul(
                                pb[6][0:16, 0:PH],
                                selG_s[0:16, (t2 * 4 + tt) * 16:
                                       (t2 * 4 + tt + 1) * 16],
                                gate_sb[0:16, tt * PH:(tt + 1) * PH],
                                start=(tt == 0), stop=(tt == 3))
                        return r
                    pev = P.t(pmm, waits=[("act", P.c["act"]),
                                          ("dve", P.c["dve"])])
                    P.v(lambda v, t2=t2: v.tensor_copy(
                        gate_r[:, t2 * PH:(t2 + 1) * PH], pb[6][0:16, 0:PH]),
                        waits=[("pe", pev)])
                P.v(lambda v: v.tensor_scalar_mul(gate_r[:], gate_r[:],
                                                  gra_s[:, 0:1]))
                P.barrier()

                dvv = P.c["dve"]
                P.emit("y", lambda y: y.dma_start(
                    out=k_swap[0:64, :], in_=k_sb[64:128, :]),
                    inc="ld", inc_n=16, waits=[("dve", dvv)])
                klv = P.emit("y", lambda y: y.dma_start(
                    out=k_swap[64:128, :], in_=k_sb[0:64, :]),
                    inc="ld", inc_n=16)
                for eng in ("t", "v"):
                    P.emit(eng, None, waits=[("ld", klv)])

                def kv_ap(buf_nat, buf_sw, bias_nat, bias_sw, r, s2, half):
                    Gk = 5 * r + s2
                    sk, hk = Gk // 16, Gk % 16
                    if sk < 4:
                        buf = buf_nat if (hk % 2) == half else buf_sw
                        return buf[64 * half:64 * half + 64,
                                   (hk // 2) * TA + sk * PH:
                                   (hk // 2) * TA + (sk + 1) * PH]
                    col = bias_nat if (hk % 2) == half else bias_sw
                    return bass.AP(col, 64 * half * 8 + hk // 2,
                                   [[8, 64], [0, PH]])

                evac = [0, 0]
                pe_hist = [0, 0, 0, 0]
                for t2 in range(4):
                    u = t2 % 2
                    for s2 in range(SRC):
                        bank = (t2 * SRC + s2) % 2
                        pe_red = 0
                        for r in range(16):
                            G = 4 * r + t2
                            tq, hq = G // 16, G % 16
                            qap = q_sb[64 * u:64 * u + 64,
                                       (hq // 2) * TA + tq * PH:
                                       (hq // 2) * TA + (tq + 1) * PH]
                            kap = kv_ap(k_sb, k_swap, bkc_bf, bkc_sw,
                                        r, s2, u)
                            pg = r % 4
                            psl = slice(pg * PH, (pg + 1) * PH)
                            use_pool = (r % 3 == 2)
                            peng = P.g if use_pool else P.v
                            pctr = "pool" if use_pool else "dve"
                            dv = peng(lambda e, qap=qap, kap=kap, psl=psl,
                                      u=u: e.tensor_tensor(
                                prod[64 * u:64 * u + 64, psl], qap, kap,
                                ALU.mult),
                                waits=[("pe", pe_hist[pg])]
                                if pe_hist[pg] else [])
                            w = [(pctr, dv)]
                            if r == 0:
                                w.append(("act", evac[bank]))
                            pe_red = P.t(lambda t, r=r, bank=bank, psl=psl,
                                         u=u: t.matmul(
                                pb[bank][0:16, 0:PH],
                                selRed_s[64 * u:64 * u + 64,
                                         r * 16:(r + 1) * 16],
                                prod[64 * u:64 * u + 64, psl],
                                start=(r == 0), stop=(r == 15)),
                                waits=w)
                            pe_hist[pg] = pe_red
                        evac[bank] = P.s(
                            lambda s, t2=t2, s2=s2, bank=bank: s.copy(
                                sc0[:, (t2 * SRC + s2) * PH:
                                    (t2 * SRC + s2 + 1) * PH],
                                pb[bank][0:16, 0:PH]),
                            waits=[("pe", pe_red)])
                P.barrier()

                for t2 in range(BATCH):
                    gateap = bass.AP(gate_r, t2 * PH,
                                     [[TA, 16], [0, SRC], [1, PH]])
                    pbap = bass.AP(pb_sb, t2 * SRC,
                                   [[20, 16], [1, SRC], [0, PH]])
                    dv = P.v(lambda v, gateap=gateap, pbap=pbap:
                             v.tensor_tensor(
                        gp[:].rearrange("h (s o) -> h s o", s=SRC),
                        gateap, pbap, ALU.mult))
                    dv = P.v(lambda v, t2=t2: v.scalar_tensor_tensor(
                        sc0[:, t2 * SRC * PH:(t2 + 1) * SRC * PH],
                        sc0[:, t2 * SRC * PH:(t2 + 1) * SRC * PH],
                        1.0 / 256.0, gp[:], ALU.mult, ALU.add))
                s_sl = lambda j: bass.AP(sc0, j * PH,
                                         [[SRC * TA, 16], [SRC * PH, BATCH],
                                          [1, PH]])
                m_ap = gate_sb[:].rearrange("h (t o) -> h t o", t=BATCH)
                dv = P.v(lambda v: v.tensor_tensor(
                    m_ap, s_sl(0), s_sl(1), ALU.max))
                for j in (2, 3, 4):
                    dv = P.v(lambda v, j=j: v.tensor_tensor(
                        m_ap, m_ap, s_sl(j), ALU.max))
                map_b = bass.AP(gate_sb, 0, [[TA, 16], [PH, BATCH],
                                             [0, SRC], [1, PH]])
                dv = P.v(lambda v: v.tensor_tensor(
                    sc0[:].rearrange("h (t s o) -> h t s o", t=BATCH, s=SRC),
                    sc0[:].rearrange("h (t s o) -> h t s o", t=BATCH, s=SRC),
                    map_b, ALU.subtract))
                av = P.s(lambda s: s.activation(sc0[:], sc0[:], AF.Exp),
                         waits=[("dve", dv)])
                sum_ap = gp[:, 0:TA].rearrange("h (t o) -> h t o", t=BATCH)
                dv = P.v(lambda v: v.tensor_tensor(
                    sum_ap, s_sl(0), s_sl(1), ALU.add), waits=[("act", av)])
                for j in (2, 3, 4):
                    dv = P.v(lambda v, j=j: v.tensor_tensor(
                        sum_ap, sum_ap, s_sl(j), ALU.add))
                dv = P.v(lambda v: v.reciprocal(gp[:, 0:TA], gp[:, 0:TA]))
                rap = bass.AP(gp, 0, [[SRC * PH, 16], [PH, BATCH],
                                      [0, SRC], [1, PH]])
                dv = P.v(lambda v: v.tensor_tensor(
                    probs[:].rearrange("h (t s o) -> h t s o", t=BATCH,
                                       s=SRC),
                    sc0[:].rearrange("h (t s o) -> h t s o", t=BATCH, s=SRC),
                    rap, ALU.mult))
                P.barrier()

                load_w(vw_in)
                proj(vbc, k_sb)
                P.barrier()
                dvv = P.c["dve"]
                P.emit("y", lambda y: y.dma_start(
                    out=k_swap[0:64, :], in_=k_sb[64:128, :]),
                    inc="ld", inc_n=16, waits=[("dve", dvv),
                                               ("act", P.c["act"])])
                klv = P.emit("y", lambda y: y.dma_start(
                    out=k_swap[64:128, :], in_=k_sb[0:64, :]),
                    inc="ld", inc_n=16)
                for eng in ("t", "v"):
                    P.emit(eng, None, waits=[("ld", klv)])

                ev_s = [0, 0, 0, 0]
                ev_rd = [(None, 0)] * 4
                prod_rd = {2: 0, 3: 0}
                cnt = 0
                for t2 in range(BATCH):
                    for r in range(16):
                        half = r % 2
                        accsl = acc[64 * half:64 * half + 64,
                                    (r // 2) * TA + t2 * PH:
                                    (r // 2) * TA + (t2 + 1) * PH]
                        prev_mult = (None, 0)
                        prev_add = 0
                        for s2 in range(SRC):
                            bank = cnt % 4
                            slot = cnt % 4
                            cnt += 1
                            pev = P.t(lambda t, t2=t2, s2=s2, r=r, half=half,
                                      bank=bank: t.matmul(
                                pb[bank][64 * half:64 * half + 64, 0:PH],
                                selB_s[0:16, r * 64:(r + 1) * 64],
                                probs[:, (t2 * SRC + s2) * PH:
                                      (t2 * SRC + s2 + 1) * PH],
                                start=True, stop=True),
                                waits=[("act", ev_s[bank])])
                            sw = [("pe", pev)]
                            if ev_rd[slot][0] is not None:
                                sw.append(ev_rd[slot])
                            ev_s[bank] = P.s(
                                lambda s, bank=bank, half=half, slot=slot:
                                s.copy(
                                    pbc[64 * half:64 * half + 64,
                                        slot * PH:(slot + 1) * PH],
                                    pb[bank][64 * half:64 * half + 64, 0:PH]),
                                waits=sw)
                            vap = kv_ap(k_sb, k_swap, bvc_bf, bvc_sw,
                                        r, s2, half)
                            use_pool = s2 in (1, 3)
                            eng = P.g if use_pool else P.v
                            ctr = "pool" if use_pool else "dve"
                            if s2 == 0:
                                mv = P.v(
                                    lambda v, accsl=accsl, vap=vap,
                                    half=half, slot=slot: v.tensor_tensor(
                                        accsl,
                                        pbc[64 * half:64 * half + 64,
                                            slot * PH:(slot + 1) * PH],
                                        vap, ALU.mult),
                                    waits=[("act", ev_s[bank])])
                                ev_rd[slot] = ("dve", mv)
                                prev_add = mv
                            else:
                                pslot = 2 + s2 % 2
                                psl = slice(pslot * PH, (pslot + 1) * PH)
                                mw = [("act", ev_s[bank])]
                                if prod_rd[pslot]:
                                    mw.append(("dve", prod_rd[pslot]))
                                mv = eng(lambda e, vap=vap, half=half,
                                         slot=slot, psl=psl:
                                         e.tensor_tensor(
                                        prod[64 * half:64 * half + 64, psl],
                                        pbc[64 * half:64 * half + 64,
                                            slot * PH:(slot + 1) * PH],
                                        vap, ALU.mult), waits=mw)
                                ev_rd[slot] = (ctr, mv)
                                aw = [(ctr, mv)]
                                mv2 = P.v(
                                    lambda v, accsl=accsl, half=half,
                                    psl=psl: v.tensor_tensor(
                                        accsl, accsl,
                                        prod[64 * half:64 * half + 64, psl],
                                        ALU.add), waits=aw)
                                prod_rd[pslot] = mv2
                                prev_add = mv2
                            prev_mult = (ctr, mv)
                P.barrier()
                if _UPTO < 4:
                    continue
                load_w(ow_in)
                for tg in range(2):
                    prev_stt = [0, 0]
                    for oc in range(DC):
                        bank = oc % 2

                        def mm(t, oc=oc, bank=bank, tg=tg):
                            r = None
                            for icc in range(DC):
                                r = t.matmul(
                                    pb[bank][:, 0:T],
                                    wslot[:, icc * D + oc * 128:
                                          icc * D + (oc + 1) * 128],
                                    acc[:, icc * TA + tg * T:
                                        icc * TA + (tg + 1) * T],
                                    start=(icc == 0), stop=(icc == DC - 1))
                            return r
                        pev = P.t(mm, waits=[("dve", prev_stt[bank])])
                        ylv = P.emit("y", lambda y, oc=oc, tg=tg,
                                     ph=ph: y.dma_start(
                            out=x2t[:, 0:T].rearrange(
                                "p (a o) -> p a o", a=2),
                            in_=bass.AP(y0_in.tensor,
                                        oc * 128 * TOK + tg * 2 * T + ph * PH,
                                        [[TOK, 128], [T, 2], [1, PH]])),
                            inc="ld", inc_n=16,
                            waits=[("dve", P.c["dve"])])
                        av = P.s(lambda s, oc=oc, bank=bank: s.activation(
                            tmp1[:, 0:T], pb[bank][:, 0:T],
                            AF.Identity, bias=obc[:, oc:oc + 1]),
                            waits=[("pe", pev), ("dve", P.c["dve"])])
                        prev_stt[bank] = P.v(
                            lambda v, oc=oc: v.tensor_tensor(
                                y1g[:, oc * T:(oc + 1) * T], tmp1[:, 0:T],
                                x2t[:, 0:T], ALU.add),
                            waits=[("act", av), ("ld", ylv)])
                    P.barrier()
                    if _UPTO < 420:
                        continue
                    ln_fm_at(y1g, 0, y1g, 0, T_=T)
                    P.barrier()
                    if _UPTO < 430:
                        continue
                    for oc in range(DC):
                        P.emit("y", lambda y, oc=oc, tg=tg, ph=ph:
                               y.dma_start(
                            out=bass.AP(y1_dram.tensor,
                                        oc * 128 * TOK + tg * 2 * T + ph * PH,
                                        [[TOK, 128], [T, 2], [1, PH]]),
                            in_=y1g[:, oc * T:(oc + 1) * T].rearrange(
                                "p (a o) -> p a o", a=2)),
                            inc="st", inc_n=16,
                            waits=[("dve", P.c["dve"])] if oc == 0 else [])
                    P.barrier()

        P.barrier()

        with ExitStack() as bctx:
            if _UPTO < 600:
                bctx = bctx
            (a_sb, b_sb, a1, wbuf, y1f, ovl) = _cm(
                bctx,
                nc.sbuf_tensor("a_sb", [128, 32 * TA], BF16),
                nc.sbuf_tensor("b_sb", [128, 32 * TA], BF16),
                nc.sbuf_tensor("a1", [128, T], BF16),
                nc.sbuf_tensor("wbuf", [128, 2 * 32 * 128], BF16),
                nc.sbuf_tensor("y1f", [128, 2 * T], F32),
                nc.sbuf_tensor("ovl", [128, DC * TA], F32),
            )
            res = ovl
            y1bf = ovl[:].bitcast(BF16)[:, 0:DC * TA]
            for hf in range(2 if _UPTO >= 600 else 0):
                col0 = hf * TA
                lv = P.emit("g", lambda g, col0=col0: g.dma_start(
                    out=y1bf[:].rearrange("p (c n) -> p c n", c=DC),
                    in_=bass.AP(y1_dram.tensor, col0,
                                [[TOK, 128], [128 * TOK, DC], [1, TA]])),
                    inc="wld", inc_n=16)
                for eng in ("t", "v", "s", "y"):
                    P.emit(eng, None, waits=[("wld", lv)])

                WS = DC * 128
                pe_after = [0, 0]

                def ldA(op, slot):
                    w = [("pe", pe_after[slot])] if pe_after[slot] else []
                    lv = P.emit("g", lambda g, op=op, slot=slot: g.dma_start(
                        out=wbuf[:, slot * 2 * WS:slot * 2 * WS + WS],
                        in_=bass.AP(w1t_in.tensor, op * 128 * D,
                                    [[D, 128], [1, D]])),
                        inc="wld", inc_n=16, waits=w)
                    return P.emit("g", lambda g, op=op, slot=slot:
                                  g.dma_start(
                        out=wbuf[:, slot * 2 * WS + WS:(slot + 1) * 2 * WS],
                        in_=bass.AP(w2t_in.tensor, op * 128 * D,
                                    [[D, 128], [1, D]])),
                        inc="wld", inc_n=16)
                lvs = {0: ldA(0, 0)}
                bank_act = [0] * 4
                bank_dve = [0] * 4
                for op in range(32):
                    slot = op % 2
                    if op + 1 < 32:
                        lvs[op + 1] = ldA(op + 1, (op + 1) % 2)
                    b1s = (2 * slot, 2 * slot + 1)
                    b2s = (4 + 2 * slot, 4 + 2 * slot + 1)

                    def mm1(t, op=op, slot=slot, b1s=b1s, b2s=b2s):
                        r = None
                        for icc in range(DC):
                            for tg in range(2):
                                r = t.matmul(
                                    pb[b1s[tg]][:, 0:T],
                                    wbuf[:, slot * 2 * WS + icc * 128:
                                         slot * 2 * WS + (icc + 1) * 128],
                                    y1bf[:, icc * TA + tg * T:
                                         icc * TA + (tg + 1) * T],
                                    start=(icc == 0), stop=(icc == DC - 1))
                        for icc in range(DC):
                            for tg in range(2):
                                r = t.matmul(
                                    pb[b2s[tg]][:, 0:T],
                                    wbuf[:, slot * 2 * WS + WS + icc * 128:
                                         slot * 2 * WS + WS + (icc + 1) * 128],
                                    y1bf[:, icc * TA + tg * T:
                                         icc * TA + (tg + 1) * T],
                                    start=(icc == 0), stop=(icc == DC - 1))
                        return r
                    w = [("wld", lvs[op])]
                    for bk in b1s:
                        if bank_act[bk]:
                            w.append(("act", bank_act[bk]))
                    for bk in b2s:
                        if bank_dve[bk % 4]:
                            w.append(("dve", bank_dve[bk % 4]))
                    pev = P.t(mm1, waits=w)
                    for tg in range(2):
                        av = P.s(lambda s, op=op, tg=tg, b1=b1s[tg]:
                                 s.activation(
                            a1[:, 0:T], pb[b1][:, 0:T],
                            AF.Silu, scale=w1s[:, op:op + 1]),
                            waits=[("pe", pev), ("dve", P.c["dve"])])
                        bank_act[b1s[tg]] = av
                        bank_dve[b2s[tg] % 4] = P.v(
                            lambda v, op=op, tg=tg, b2=b2s[tg]:
                            v.scalar_tensor_tensor(
                                a_sb[:, op * TA + tg * T:
                                     op * TA + (tg + 1) * T],
                                pb[b2][:, 0:T], w2s[:, op:op + 1],
                                a1[:, 0:T], ALU.mult, ALU.mult),
                            waits=[("act", av)])
                    pe_after[slot] = P.c["pe"]
                P.barrier()

                for (src_sb, dst_sb, h_in, h_s) in (
                        (a_sb, b_sb, h1t_in, h1s),
                        (b_sb, a_sb, h2t_in, h2s)):
                    pe_after = [0, 0]

                    def ldH(op, slot, h_in=h_in):
                        w = [("pe", pe_after[slot])] if pe_after[slot] else \
                            [("pe", P.c["pe"])]
                        return P.emit("g", lambda g, op=op, slot=slot:
                                      g.dma_start(
                            out=wbuf[:, slot * 4096:(slot + 1) * 4096],
                            in_=bass.AP(h_in.tensor, op * 128 * 4096,
                                        [[4096, 128], [1, 4096]])),
                            inc="wld", inc_n=16, waits=w)
                    lvs = {0: ldH(0, 0)}
                    bank_act = [0] * 4
                    for op in range(32):
                        slot = op % 2
                        if op + 1 < 32:
                            lvs[op + 1] = ldH(op + 1, (op + 1) % 2)
                        bks = (2 * slot, 2 * slot + 1)

                        def mmh(t, op=op, slot=slot, src_sb=src_sb, bks=bks):
                            r = None
                            for icc in range(32):
                                for tg in range(2):
                                    r = t.matmul(
                                        pb[bks[tg]][:, 0:T],
                                        wbuf[:, slot * 4096 + icc * 128:
                                             slot * 4096 + (icc + 1) * 128],
                                        src_sb[:, icc * TA + tg * T:
                                               icc * TA + (tg + 1) * T],
                                        start=(icc == 0), stop=(icc == 31))
                            return r
                        w = [("wld", lvs[op])]
                        for bk in bks:
                            if bank_act[bk]:
                                w.append(("act", bank_act[bk]))
                        pev = P.t(mmh, waits=w)
                        for tg in range(2):
                            bank_act[bks[tg]] = P.s(
                                lambda s, op=op, tg=tg, dst_sb=dst_sb,
                                h_s=h_s, bk=bks[tg]: s.activation(
                                    dst_sb[:, op * TA + tg * T:
                                           op * TA + (tg + 1) * T],
                                    pb[bk][:, 0:T], AF.Gelu,
                                    scale=h_s[:, op:op + 1]),
                                waits=[("pe", pev)])
                        pe_after[slot] = P.c["pe"]
                    P.barrier()

                pe_after = [0, 0]

                def ldD(oc, slot):
                    w = [("pe", pe_after[slot])] if pe_after[slot] else \
                        [("pe", P.c["pe"])]
                    return P.emit("g", lambda g, oc=oc, slot=slot:
                                  g.dma_start(
                        out=wbuf[:, slot * 4096:(slot + 1) * 4096],
                        in_=bass.AP(w3t_in.tensor, oc * 128 * 4096,
                                    [[4096, 128], [1, 4096]])),
                        inc="wld", inc_n=16, waits=w)
                lvs = {0: ldD(0, 0)}
                bank_dve = [0] * 4
                for oc in range(DC):
                    slot = oc % 2
                    if oc + 1 < DC:
                        lvs[oc + 1] = ldD(oc + 1, (oc + 1) % 2)
                    bks = (2 * slot, 2 * slot + 1)
                    ylv = P.emit("y", lambda y, oc=oc, col0=col0:
                                 y.dma_start(
                        out=y1f[:].rearrange("p (a o) -> p a o", a=2),
                        in_=bass.AP(y1_dram.tensor,
                                    oc * 128 * TOK + col0,
                                    [[TOK, 128], [T, 2], [1, T]])),
                        inc="ld", inc_n=16,
                        waits=[("dve", P.c["dve"])])

                    def mm3(t, oc=oc, slot=slot, bks=bks):
                        r = None
                        for icc in range(32):
                            for tg in range(2):
                                r = t.matmul(
                                    pb[bks[tg]][:, 0:T],
                                    wbuf[:, slot * 4096 + icc * 128:
                                         slot * 4096 + (icc + 1) * 128],
                                    a_sb[:, icc * TA + tg * T:
                                         icc * TA + (tg + 1) * T],
                                    start=(icc == 0), stop=(icc == 31))
                        return r
                    w = [("wld", lvs[oc])]
                    for bk in bks:
                        if bank_dve[bk]:
                            w.append(("dve", bank_dve[bk]))
                    pev = P.t(mm3, waits=w)
                    for tg in range(2):
                        bank_dve[bks[tg]] = P.v(
                            lambda v, oc=oc, tg=tg, bk=bks[tg]:
                            v.scalar_tensor_tensor(
                                res[:, oc * TA + tg * T:
                                    oc * TA + (tg + 1) * T],
                                pb[bk][:, 0:T], w3s[:, oc:oc + 1],
                                y1f[:, tg * T:(tg + 1) * T], ALU.mult,
                                ALU.add),
                            waits=[("pe", pev), ("ld", ylv)])
                    pe_after[slot] = P.c["pe"]
                P.barrier()
                for tg in range(2):
                    ln_fm_at(res, tg * T, res, tg * T, T_=T)
                    P.barrier()
                def sto(y, col0=col0):
                    return y.dma_start(
                        out=bass.AP(out_dram.tensor, col0,
                                    [[TOK, 128], [128 * TOK, DC], [1, TA]]),
                        in_=res[:].rearrange("p (c n) -> p c n", c=DC))
                P.emit("y", sto, inc="st", inc_n=16,
                       waits=[("dve", P.c["dve"])])
                P.barrier()
                for eng in ("t", "v", "s"):
                    P.emit(eng, None, waits=[("st", P.c["st"])])

        P.emit("y", lambda y: y.dma_start(out=y1_dbg, in_=y1_dram),
               inc="st", inc_n=16)
        P.emit("y", None, waits=[("st", P.c["st"])])
        P.finalize(nc, block, sems)
    return nc


def _l1_inmaps(inp, x2_shards):
    qw = np.asarray(inp["L1_attn_q_w"], np.float32)
    kw = np.asarray(inp["L1_attn_k_w"], np.float32)
    vw = np.asarray(inp["L1_attn_v_w"], np.float32)
    ow = np.asarray(inp["L1_attn_out_w"], np.float32)
    rel = np.asarray(inp["L1_attn_rel_emb"], np.float32)
    grw = np.asarray(inp["L1_attn_grep_w"], np.float32)
    grb = np.asarray(inp["L1_attn_grep_b"], np.float32).reshape(1, 8)
    gra = np.asarray(inp["L1_attn_grep_a"], np.float32).reshape(16, 1)
    bk = np.asarray(inp["L1_attn_bias_k"], np.float32).reshape(-1)
    bv = np.asarray(inp["L1_attn_bias_v"], np.float32).reshape(-1)
    def _tileT(w):
        N, K = w.shape
        n_op, n_c = N // 128, K // 128
        a = w.reshape(n_op, 128, n_c, 128)
        a = a.transpose(0, 3, 2, 1)
        return np.ascontiguousarray(a).reshape(n_op * 128, K)

    w1t = _tileT(np.asarray(inp["L1_ffn_w1_w"], np.float32))
    w2t = _tileT(np.asarray(inp["L1_ffn_w2_w"], np.float32))
    h1t = _tileT(np.asarray(inp["L1_ffn_h1_w"], np.float32))
    h2t = _tileT(np.asarray(inp["L1_ffn_h2_w"], np.float32))
    w3t = _tileT(np.asarray(inp["L1_ffn_w3_w"], np.float32))

    onehot = np.zeros((32, 20), np.float32)
    for t_ in range(4):
        for sp in range(5):
            rp = sp - t_
            bkt = (16 if rp > 0 else 0) + abs(rp)
            onehot[bkt, t_ * 5 + sp] = 1.0
    P2 = np.zeros((64, 128), np.float32)
    for p in range(128):
        P2[p % 64, p] = 1.0
    Eb = np.zeros((128, 16 * DC), np.float32)
    for c in range(DC):
        for p in range(128):
            h = 2 * c + p // 64
            Eb[p, c * 16 + h] = 1.0
    selRed = np.zeros((128, 256), np.float32)
    for r in range(16):
        selRed[:, r * 16 + r] = 1.0
    selB = np.zeros((16, 1024), np.float32)
    for r in range(16):
        selB[r, r * 64:(r + 1) * 64] = 1.0
    selG = np.zeros((16, 256), np.float32)
    for t2 in range(4):
        for tt in range(4):
            for h in range(16):
                if h % 4 == t2:
                    selG[h, (t2 * 4 + tt) * 16 + 4 * tt + h // 4] = 1.0
    onesr = np.ones((1, 128), np.float32)
    cpool = np.zeros((128, 8), np.float32)
    cpool[:, 0] = -0.5
    cpool[:, 1] = LN_EPS
    cpool[:, 3] = 1.0

    base = {
        "qw": qw, "kw": kw, "vw": vw, "ow": ow,
        "qb_c": _cols(np.asarray(inp["L1_attn_q_b"], np.float32)),
        "kb_c": _cols(np.asarray(inp["L1_attn_k_b"], np.float32)),
        "vb_c": _cols(np.asarray(inp["L1_attn_v_b"], np.float32)),
        "ob_c": _cols(np.asarray(inp["L1_attn_out_b"], np.float32)),
        "bk_c": _cols(bk), "bv_c": _cols(bv),
        "rel_emb": rel, "grep_w": grw, "grep_b": grb, "grep_a": gra,
        "onehot": onehot, "P2": P2, "E_big": Eb.astype(BF),
        "selRed": selRed.astype(BF), "selB": selB.astype(BF), "selG": selG,
        "w1t": w1t, "w2t": w2t, "h1t": h1t, "h2t": h2t, "w3t": w3t,
        "w1s_c": _cols(np.asarray(inp["L1_ffn_w1_s"], np.float32)),
        "w2s_c": _cols(np.asarray(inp["L1_ffn_w2_s"], np.float32)),
        "h1s_c": _cols(np.asarray(inp["L1_ffn_h1_s"], np.float32)),
        "h2s_c": _cols(np.asarray(inp["L1_ffn_h2_s"], np.float32)),
        "w3s_c": _cols(np.asarray(inp["L1_ffn_w3_s"], np.float32)),
        "ones_r": onesr, "cpool": cpool,
    }
    maps = []
    for c in range(8):
        h, cc = c // 4, c % 4
        y0 = np.concatenate(
            [x2_shards[2 * t_ + h][:, 512 * cc:512 * (cc + 1)]
             for t_ in range(4)], axis=1)
        m = dict(base)
        m["y0_fm"] = np.ascontiguousarray(y0)
        maps.append(m)
    return maps


_L1_CACHE = {}


def run_l1(inp, x2_shards):
    if "l1" not in _L1_CACHE:
        _L1_CACHE["l1"] = build_l1()
    nc = _L1_CACHE["l1"]
    res = run_bass_kernel_spmd(nc, _l1_inmaps(inp, x2_shards),
                               core_ids=list(range(8)))
    return [r["out_fm"] for r in res.results], [r["y1_dbg"] for r in res.results]


def kernel(**inp):
    x2 = run_l0(inp)
    outs, _y1 = run_l1(inp, x2)
    final = np.empty((4, 4096, 1024), np.float32)
    for c in range(8):
        blk = outs[c].reshape(1024, 4, 512).transpose(1, 2, 0)
        final[:, 512 * c:512 * (c + 1), :] = blk
    return final


# version 09
# speedup vs baseline: 1.2808x; 1.2808x over previous
import numpy as np
import ml_dtypes
import concourse.bass as bass
import concourse.mybir as mybir
from concourse.bass_utils import run_bass_kernel_spmd

F32 = mybir.dt.float32
BF16 = mybir.dt.bfloat16
AF = mybir.ActivationFunctionType
ALU = mybir.AluOpType
AX = mybir.AxisListType
BF = ml_dtypes.bfloat16

D = 1024
DC = 8
TOK = 2048
NG = 4
T = 512
TC = 4
LN_EPS = 1e-5
DEN_EPS = 1e-6
NH, HD, SRC, BATCH = 16, 64, 5, 4

ENGS = ("t", "v", "s", "g", "y")
CTRS = ("pe", "dve", "act", "pool", "ld", "wld", "st")
_OWN = {"t": "pe", "v": "dve", "s": "act", "g": "pool"}


class Prog:

    def __init__(self):
        self.streams = {e: [] for e in ENGS}
        self.c = {n: 0 for n in CTRS}
        self._last_bar = {n: 0 for n in CTRS}

    def emit(self, eng, fn, inc=None, inc_n=1, waits=()):
        if inc is None and eng in _OWN:
            pass
        w = [(n, v) for (n, v) in waits if v > 0]
        self.streams[eng].append((fn, w, inc, inc_n))
        if inc is not None:
            self.c[inc] += inc_n
            return self.c[inc]
        return None

    def t(self, fn, **kw):
        return self.emit("t", fn, inc="pe", **kw)

    def v(self, fn, **kw):
        return self.emit("v", fn, inc="dve", **kw)

    def s(self, fn, **kw):
        return self.emit("s", fn, inc="act", **kw)

    def g(self, fn, **kw):
        return self.emit("g", fn, inc="pool", **kw)

    def barrier(self):
        vals = [(n, self.c[n]) for n in CTRS if self.c[n] > 0]

        def mk(eng):
            own = _OWN.get(eng)
            w = [(n, v) for (n, v) in vals if n != own]
            self.streams[eng].append((None, w, None, 0))
        for eng in ENGS:
            mk(eng)
        self._last_bar = dict(self.c)

    def finalize(self, nc, block, sems):
        waited = {n: set() for n in CTRS}
        for eng in ENGS:
            for fn, waits, inc, inc_n in self.streams[eng]:
                for (n, val) in waits:
                    waited[n].add(val)
        for n in CTRS:
            if self.c[n] > 0:
                waited[n].add(self.c[n])
        for eng in ENGS:
            entries = self.streams[eng]
            fire = [0] * len(entries)
            pend = {}
            run = {n: 0 for n in CTRS}
            for idx, (fn, waits, inc, inc_n) in enumerate(entries):
                if waits:
                    for n, (amt, li) in list(pend.items()):
                        fire[li] += amt
                        del pend[n]
                if fn is None or inc is None:
                    continue
                run[inc] += inc_n
                if inc in ("ld", "wld", "st"):
                    fire[idx] += inc_n
                    continue
                amt, _ = pend.get(inc, (0, None))
                amt += inc_n
                if run[inc] in waited[inc]:
                    fire[idx] += amt
                    pend.pop(inc, None)
                else:
                    pend[inc] = (amt, idx)
            for n, (amt, li) in pend.items():
                fire[li] += amt

            sec = {"t": block.tensor, "v": block.vector, "s": block.scalar,
                   "g": block.gpsimd, "y": block.sync}[eng]

            def body(handle, eng=eng, entries=entries, fire=fire):
                for idx, (fn, waits, inc, inc_n) in enumerate(entries):
                    for (n, val) in waits:
                        handle.wait_ge(sems[n], val)
                    if fn is None:
                        continue
                    inst = fn(handle)
                    if fire[idx]:
                        inst.then_inc(sems[inc], fire[idx])
            sec(body)


def _cm(ctx, *mgrs):
    return [ctx.enter_context(m) for m in mgrs]



def build_l0():
    import os as _os0
    _L0UP = int(_os0.environ.get("L0_UPTO", "9999"))
    nc = bass.Bass("TRN2", target_bir_lowering=False)

    x_in = nc.dram_tensor("x_fm", [D, TOK], F32, kind="ExternalInput").ap()
    nonce_in = nc.dram_tensor("nonce", [1, 4], mybir.dt.int32,
                              kind="ExternalInput").ap()
    xnd_t = nc.dram_tensor("xn_d", [NG * 128, DC * T], BF16)
    kfmd_t = nc.dram_tensor("kfm_d", [NG * 128, DC * T], BF16)
    vtmd_t = nc.dram_tensor("vtm_d", [NG * 128, TC * D], BF16)
    spill_t = nc.dram_tensor("spill_d", [3 * 128, DC * D], BF16)
    kspill_t = nc.dram_tensor("kspill_d", [3 * 128, DC], F32)
    shS_t = nc.dram_tensor("shS", [2 * 128, DC * D], F32, addr_space="Shared")
    shK_t = nc.dram_tensor("shK", [2 * 128, DC], F32, addr_space="Shared")
    shF_t = nc.dram_tensor("shF", [2, 16], mybir.dt.int32,
                           addr_space="Shared")
    qkvw_in = nc.dram_tensor("qkv_w", [D, 3 * D], F32, kind="ExternalInput").ap()
    gatew_in = nc.dram_tensor("gate_w", [D, D], F32, kind="ExternalInput").ap()
    w0t_in = nc.dram_tensor("w0t", [D, D], F32, kind="ExternalInput").ap()
    qkvb_c_in = nc.dram_tensor("qkv_b_c", [128, 24], F32, kind="ExternalInput").ap()
    qkvb_r_in = nc.dram_tensor("qkv_b_r", [1, 3 * D], F32, kind="ExternalInput").ap()
    gateb_c_in = nc.dram_tensor("gate_b_c", [128, 8], F32, kind="ExternalInput").ap()
    lng_c_in = nc.dram_tensor("ln_g_c", [128, 8], F32, kind="ExternalInput").ap()
    lnb_c_in = nc.dram_tensor("ln_b_c", [128, 8], F32, kind="ExternalInput").ap()
    hmask_in = nc.dram_tensor("hmask", [128, 1], F32, kind="ExternalInput").ap()
    ident_in = nc.dram_tensor("ident16", [128, 128], BF16, kind="ExternalInput").ap()
    triu_in = nc.dram_tensor("triu", [128, 128], BF16, kind="ExternalInput").ap()
    onesr_in = nc.dram_tensor("ones_r", [1, 128], F32, kind="ExternalInput").ap()
    cpool_in = nc.dram_tensor("cpool", [128, 8], F32, kind="ExternalInput").ap()

    x2_out = nc.dram_tensor("x2", [D, TOK], F32, kind="ExternalOutput").ap()

    from contextlib import ExitStack
    ctx = ExitStack()
    with ctx:
        (qkvw, qkvb_cs,
         qkvb_rs_bf, gateb_cs, lng_cs, lnb_cs, hmask_s, ident, triu, onesr,
         onesr_bf, onesc, onesc_bf, cpool, nonce_sb, flag_sb) = _cm(
            ctx,
            nc.sbuf_tensor("qkvw", [128, DC * 3 * D], BF16),
            nc.sbuf_tensor("qkvb_cs", [128, 24], F32),
            nc.sbuf_tensor("qkvb_rs_bf", [1, 3 * D], BF16),
            nc.sbuf_tensor("gateb_cs", [128, 8], F32),
            nc.sbuf_tensor("lng_cs", [128, 8], F32),
            nc.sbuf_tensor("lnb_cs", [128, 8], F32),
            nc.sbuf_tensor("hmask_s", [128, 1], F32),
            nc.sbuf_tensor("ident_s", [128, 128], BF16),
            nc.sbuf_tensor("triu_s", [128, 128], BF16),
            nc.sbuf_tensor("onesr", [1, 128], F32),
            nc.sbuf_tensor("onesr_bf", [1, 128], BF16),
            nc.sbuf_tensor("onesc", [128, 1], F32),
            nc.sbuf_tensor("onesc_bf", [128, 1], BF16),
            nc.sbuf_tensor("cpool_s", [128, 8], F32),
            nc.sbuf_tensor("nonce_sb", [1, 4], mybir.dt.int32),
            nc.sbuf_tensor("flag_sb", [1, 4], mybir.dt.int32),
        )
        (xg, scratch, xn, q_bf, kfm, vtm, sc_sb, tmp1,
         rowbuf, S_sb, ksum, kdelta, kspl, kwork2, kwork_bf2) = _cm(
            ctx,
            nc.sbuf_tensor("xg", [128, DC * T], F32),
            nc.sbuf_tensor("scratch", [128, DC * T], F32),
            nc.sbuf_tensor("xn", [128, DC * T], BF16),
            nc.sbuf_tensor("q_bf", [128, DC * T], BF16),
            nc.sbuf_tensor("kfm", [128, DC * T], BF16),
            nc.sbuf_tensor("vtm", [128, TC * D], BF16),
            nc.sbuf_tensor("sc_sb", [128, TC * T], BF16),
            nc.sbuf_tensor("tmp1", [128, T], F32),
            nc.sbuf_tensor("rowbuf", [1, 3 * T], F32),
            nc.sbuf_tensor("S_sb", [128, DC * D], F32),
            nc.sbuf_tensor("ksum", [128, DC], F32),
            nc.sbuf_tensor("kdelta", [128, DC], F32),
            nc.sbuf_tensor("kspl", [128, DC], F32),
            nc.sbuf_tensor("kwork2", [128, 2 * DC], F32),
            nc.sbuf_tensor("kwork_bf2", [128, 2 * DC], BF16),
        )
        xsem = ctx.enter_context(nc.semaphore("xsem"))
        gate_bf = q_bf
        pb = _cm(ctx, *[nc.psum_tensor(f"pb{i}", [128, 512], F32)
                        for i in range(8)])
        semh = _cm(ctx, *[nc.semaphore(f"sm_{n}") for n in CTRS])
        sems = dict(zip(CTRS, semh))
        block = ctx.enter_context(nc.Block())

        P = Prog()

        def ld(dst, src):
            P.emit("y", lambda y, d=dst, s=src: y.dma_start(out=d, in_=s),
                   inc="ld", inc_n=16)

        for dst, src in ((qkvb_cs[:], qkvb_c_in),
                         (gateb_cs[:], gateb_c_in), (lng_cs[:], lng_c_in),
                         (lnb_cs[:], lnb_c_in), (hmask_s[:], hmask_in),
                         (ident[:], ident_in), (triu[:], triu_in),
                         (onesr[:], onesr_in), (cpool[:], cpool_in)):
            ld(dst, src)
        ld(nonce_sb[:], nonce_in)
        for ic in range(DC):
            P.emit("g", lambda g, ic=ic: g.dma_start(
                out=qkvw[:, ic * 3 * D:(ic + 1) * 3 * D],
                in_=qkvw_in[ic * 128:(ic + 1) * 128, :]), inc="wld", inc_n=16)
        P.emit("g", lambda g: g.dma_start(out=qkvb_rs_bf[:], in_=qkvb_r_in),
               inc="wld", inc_n=16)
        P.v(lambda v: v.memset(onesc[:], 1.0))
        P.v(lambda v: v.memset(onesc_bf[:], 1.0))
        P.v(lambda v: v.memset(onesr_bf[:], 1.0))
        P.barrier()

        def ln_fm(src_sb, dst_fp32, dst_bf, gb):
            prev_pe = 0
            for icc in range(DC):
                av = P.s(lambda s, icc=icc: s.activation(
                    tmp1[:], src_sb[:, icc * T:(icc + 1) * T], AF.Square),
                    waits=[("pe", prev_pe)] if prev_pe else [])
                P.t(lambda t, icc=icc: t.matmul(
                    pb[6][0:1, 0:T], onesc[:],
                    src_sb[:, icc * T:(icc + 1) * T],
                    start=(icc == 0), stop=(icc == DC - 1)))
                prev_pe = P.t(lambda t, icc=icc: t.matmul(
                    pb[7][0:1, 0:T], onesc[:], tmp1[:],
                    start=(icc == 0), stop=(icc == DC - 1)),
                    waits=[("act", av)])
            pev = P.c["pe"]
            P.v(lambda v: v.tensor_scalar_mul(rowbuf[0:1, 0:T],
                                              pb[6][0:1, 0:T], -1.0 / D),
                waits=[("pe", pev)])
            P.v(lambda v: v.tensor_scalar_mul(rowbuf[0:1, 2 * T:3 * T],
                                              pb[7][0:1, 0:T], 1.0 / D))
            P.v(lambda v: v.tensor_tensor(rowbuf[0:1, T:2 * T],
                                          rowbuf[0:1, 0:T], rowbuf[0:1, 0:T],
                                          ALU.mult))
            dv = P.v(lambda v: v.tensor_tensor(
                rowbuf[0:1, 2 * T:3 * T], rowbuf[0:1, 2 * T:3 * T],
                rowbuf[0:1, T:2 * T], ALU.subtract))
            P.s(lambda s: s.activation(rowbuf[0:1, T:2 * T],
                                       rowbuf[0:1, 2 * T:3 * T], AF.Ln,
                                       bias=cpool[0:1, 1:2]),
                waits=[("dve", dv)])
            av = P.s(lambda s: s.activation(rowbuf[0:1, 2 * T:3 * T],
                                            rowbuf[0:1, T:2 * T], AF.Exp,
                                            scale=cpool[0:1, 0:1]))
            P.t(lambda t: t.matmul(pb[5][:, 0:T], onesr[:], rowbuf[0:1, 0:T],
                                   start=True, stop=True),
                waits=[("dve", dv)])
            pev = P.t(lambda t: t.matmul(pb[7][:, 0:T], onesr[:],
                                         rowbuf[0:1, 2 * T:3 * T], start=True,
                                         stop=True), waits=[("act", av)])
            for icc in range(DC):
                sl = slice(icc * T, (icc + 1) * T)
                w = [("pe", pev)] if icc == 0 else []
                P.v(lambda v, sl=sl: v.scalar_tensor_tensor(
                    tmp1[:], src_sb[:, sl], 1.0, pb[5][:, 0:T],
                    ALU.mult, ALU.add), waits=w)
                if gb is None:
                    dsts = [d for d in (dst_fp32, dst_bf) if d is not None]
                    P.v(lambda v, sl=sl, d0=dsts[0]: v.tensor_tensor(
                        d0[:, sl], tmp1[:], pb[7][:, 0:T], ALU.mult))
                    if len(dsts) > 1:
                        P.v(lambda v, sl=sl, d0=dsts[0], d1=dsts[1]:
                            v.tensor_copy(d1[:, sl], d0[:, sl]))
                else:
                    gcol, bcol = gb
                    P.v(lambda v, sl=sl: v.tensor_tensor(
                        tmp1[:], tmp1[:], pb[7][:, 0:T], ALU.mult))
                    dsts = [d for d in (dst_fp32, dst_bf) if d is not None]
                    P.v(lambda v, sl=sl, icc=icc, d0=dsts[0]: v.tensor_scalar(
                        d0[:, sl], tmp1[:], gcol[:, icc:icc + 1],
                        bcol[:, icc:icc + 1], ALU.mult, ALU.add))
                    if len(dsts) > 1:
                        P.v(lambda v, sl=sl, d0=dsts[0], d1=dsts[1]:
                            v.tensor_copy(d1[:, sl], d0[:, sl]))

        def proj_fm(w_sb, wrow, w_off, bias_cols, boff, dst, act_fn, src_bf):
            evac = [0, 0]
            for oc in range(DC):
                bank = oc % 2

                def mm(t, oc=oc, bank=bank):
                    r = None
                    for icc in range(DC):
                        r = t.matmul(
                            pb[bank][:, 0:T],
                            w_sb[:, icc * wrow + w_off + oc * 128:
                                 icc * wrow + w_off + (oc + 1) * 128],
                            src_bf[:, icc * T:(icc + 1) * T],
                            start=(icc == 0), stop=(icc == DC - 1))
                    return r
                pev = P.t(mm, waits=[("act", evac[bank])])
                evac[bank] = P.s(lambda s, oc=oc, bank=bank: s.activation(
                    dst[:, oc * T:(oc + 1) * T], pb[bank][:, 0:T], act_fn,
                    bias=bias_cols[:, boff + oc:boff + oc + 1]),
                    waits=[("pe", pev)])

        def elu1(src, dst, width):
            dv = P.v(lambda v: v.tensor_scalar_min(tmp1[:, 0:width], src, 0.0))
            av = P.s(lambda s: s.activation(tmp1[:, 0:width], tmp1[:, 0:width],
                                            AF.Exp), waits=[("dve", dv)])
            P.v(lambda v: v.scalar_tensor_tensor(dst, src, 0.0,
                                                 tmp1[:, 0:width], ALU.max,
                                                 ALU.add), waits=[("act", av)])

        def proj_tm(w_off3, dst_tm):
            evac = [0, 0]
            for tc in range(TC):
                for no in range(2):
                    bank = (tc * 2 + no) % 2

                    def mm(t, tc=tc, no=no, bank=bank):
                        for icc in range(DC):
                            t.matmul(
                                pb[bank][:, 0:512],
                                xn[:, icc * T + tc * 128:icc * T + (tc + 1) * 128],
                                qkvw[:, icc * 3 * D + w_off3 + no * 512:
                                     icc * 3 * D + w_off3 + (no + 1) * 512],
                                start=(icc == 0), stop=False)
                        return t.matmul(
                            pb[bank][:, 0:512], onesr_bf[0:1, 0:128],
                            qkvb_rs_bf[0:1, w_off3 + no * 512:
                                       w_off3 + (no + 1) * 512],
                            start=False, stop=True)
                    pev = P.t(mm, waits=[("dve", evac[bank])])
                    evac[bank] = P.v(
                        lambda v, tc=tc, no=no, bank=bank: v.tensor_copy(
                            dst_tm[:, tc * D + no * 512:tc * D + (no + 1) * 512],
                            pb[bank][:, 0:512]), waits=[("pe", pev)])

        with ExitStack() as p1ctx:
            (gatew, ktm) = _cm(
                p1ctx,
                nc.sbuf_tensor("gatew", [128, DC * D], BF16),
                nc.sbuf_tensor("ktm", [128, TC * D], BF16),
            )
            for ic in range(DC):
                P.emit("g", lambda g, ic=ic: g.dma_start(
                    out=gatew[:, ic * D:(ic + 1) * D],
                    in_=gatew_in[ic * 128:(ic + 1) * 128, :]),
                    inc="wld", inc_n=16)
            P.v(lambda v: v.memset(S_sb[:], 0.0))
            P.v(lambda v: v.memset(ksum[:], 0.0))
            P.barrier()

            for gidx in range(NG):
                def ldx(y, gidx=gidx):
                    return y.dma_start(
                        out=xg[:].rearrange("p (c t) -> p c t", t=T),
                        in_=x_in[:, gidx * T:(gidx + 1) * T].rearrange(
                            "(c p) t -> p c t", p=128))
                lv = P.emit("y", ldx, inc="ld", inc_n=16)
                wv = P.c["wld"]
                for eng in ("t", "v", "s"):
                    P.emit(eng, None, waits=[("ld", lv), ("wld", wv)])

                ln_fm(xg, None, xn, gb=(lng_cs, lnb_cs))
                P.barrier()
                P.emit("y", lambda y, gidx=gidx: y.dma_start(
                    out=bass.AP(xnd_t, gidx * 128 * DC * T,
                                [[DC * T, 128], [1, DC * T]]),
                    in_=xn[:]), inc="st", inc_n=16)

                proj_fm(gatew, D, 0, gateb_cs, 0, gate_bf, AF.Sigmoid, xn)
                proj_fm(qkvw, 3 * D, D, qkvb_cs, 8, scratch, AF.Identity, xn)
                P.barrier()
                for oc in range(DC):
                    sl = slice(oc * T, (oc + 1) * T)
                    P.v(lambda v, sl=sl: v.tensor_tensor(
                        scratch[:, sl], scratch[:, sl], gate_bf[:, sl],
                        ALU.mult))
                    elu1(scratch[:, sl], kfm[:, sl], T)
                P.barrier()
                P.emit("y", lambda y, gidx=gidx: y.dma_start(
                    out=bass.AP(kfmd_t, gidx * 128 * DC * T,
                                [[DC * T, 128], [1, DC * T]]),
                    in_=kfm[:]), inc="st", inc_n=16)
                proj_tm(2 * D, vtm)
                P.barrier()
                P.emit("y", lambda y, gidx=gidx: y.dma_start(
                    out=bass.AP(vtmd_t, gidx * 128 * TC * D,
                                [[TC * D, 128], [1, TC * D]]),
                    in_=vtm[:]), inc="st", inc_n=16)
                evac = [0, 0]
                for icc in range(DC):
                    for tc in range(TC):
                        bank = (icc * TC + tc) % 2
                        pev = P.t(lambda t, icc=icc, tc=tc, bank=bank:
                                  t.transpose(
                            pb[bank][:].bitcast(BF16)[:, 0:128],
                            kfm[:, icc * T + tc * 128:icc * T + (tc + 1) * 128],
                            ident[:]), waits=[("dve", evac[bank])])
                        evac[bank] = P.v(
                            lambda v, icc=icc, tc=tc, bank=bank: v.tensor_copy(
                                ktm[:, tc * D + icc * 128:tc * D + (icc + 1) * 128],
                                pb[bank][:].bitcast(BF16)[:, 0:128]),
                            waits=[("pe", pev)])
                P.barrier()
                _dgate = int(_os0.environ.get("L0_D1", "2"))
                want_delta = gidx < NG - 1 and _dgate >= 1
                want_dstore = gidx < NG - 1 and _dgate >= 2
                scr_bf = scratch[:].bitcast(BF16)
                evac = [0, 0]
                evca = [0, 0]
                for dk in range(DC):
                    for dvh in range(2):
                        bank = (dk * 2 + dvh) % 2

                        def mm(t, dk=dk, dvh=dvh, bank=bank):
                            r = None
                            for tc in range(TC):
                                r = t.matmul(
                                    pb[bank][:, 0:512],
                                    ktm[:, tc * D + dk * 128:tc * D + (dk + 1) * 128],
                                    vtm[:, tc * D + dvh * 512:tc * D + (dvh + 1) * 512],
                                    start=(tc == 0), stop=(tc == TC - 1))
                            return r
                        pev = P.t(mm, waits=[("dve", evac[bank]),
                                             ("act", evca[bank])])
                        sl = slice(dk * D + dvh * 512, dk * D + (dvh + 1) * 512)
                        evac[bank] = P.v(lambda v, sl=sl, bank=bank:
                                         v.tensor_tensor(S_sb[:, sl],
                                                         S_sb[:, sl],
                                                         pb[bank][:, 0:512],
                                                         ALU.add),
                                         waits=[("pe", pev)])
                        if want_delta:
                            evca[bank] = P.s(
                                lambda s, sl=sl, bank=bank: s.activation(
                                    scr_bf[:, sl], pb[bank][:, 0:512],
                                    AF.Identity),
                                waits=[("dve", evac[bank])])
                for dk in range(DC):
                    def mm(t, dk=dk):
                        r = None
                        for tc in range(TC):
                            r = t.matmul(
                                pb[6][:, 0:1],
                                ktm[:, tc * D + dk * 128:tc * D + (dk + 1) * 128],
                                onesc_bf[:],
                                start=(tc == 0), stop=(tc == TC - 1))
                        return r
                    pev = P.t(mm, waits=[("dve", P.c["dve"]),
                                         ("act", P.c["act"])])
                    kdv = P.v(lambda v, dk=dk: v.tensor_tensor(
                        ksum[:, dk:dk + 1], ksum[:, dk:dk + 1], pb[6][:, 0:1],
                        ALU.add), waits=[("pe", pev)])
                    if want_delta:
                        P.s(lambda s, dk=dk: s.activation(
                            kdelta[:, dk:dk + 1], pb[6][:, 0:1], AF.Identity),
                            waits=[("dve", kdv)])
                P.barrier()
                if want_dstore:
                    P.emit("y", lambda y, gidx=gidx: y.dma_start(
                        out=bass.AP(spill_t, gidx * 128 * DC * D,
                                    [[DC * D, 128], [1, DC * D]]),
                        in_=scr_bf[:, 0:DC * D]), inc="st", inc_n=16)
                    P.emit("y", lambda y, gidx=gidx: y.dma_start(
                        out=bass.AP(kspill_t, gidx * 128 * DC,
                                    [[DC, 128], [1, DC]]),
                        in_=kdelta[:]), inc="st", inc_n=16)
                    P.barrier()

            if _L0UP < 2:
                P.emit("y", None, waits=[("st", P.c["st"])])
                P.finalize(nc, block, sems)
                return nc
            def xchg(g):
                with (g.register("r_slot") as r_slot,
                      g.register("r_poff") as r_poff,
                      g.register("r_tmp") as r_tmp,
                      g.register("r_iend") as r_iend,
                      g.register("r_non") as r_non,
                      g.register("r_fc") as r_fc):
                    pid = nc.partition_id()
                    g.reg_mov(r_slot, pid)
                    g.reg_mod(r_slot, r_slot, 2)
                    g.reg_mov(r_tmp, r_slot)
                    g.reg_mul(r_tmp, r_tmp, 128 * DC * D)
                    g.dma_start(out=bass.AP(shS_t, r_tmp,
                                            [[DC * D, 128], [1, DC * D]]),
                                in_=S_sb[:]).then_inc(xsem, 16)
                    g.reg_mov(r_tmp, r_slot)
                    g.reg_mul(r_tmp, r_tmp, 128 * DC)
                    g.dma_start(out=bass.AP(shK_t, r_tmp,
                                            [[DC, 128], [1, DC]]),
                                in_=ksum[:]).then_inc(xsem, 16)
                    g.wait_ge(xsem, 32)
                    g.reg_mov(r_tmp, r_slot)
                    g.reg_mul(r_tmp, r_tmp, 16)
                    g.dma_start(out=bass.AP(shF_t, r_tmp, [[16, 1], [1, 1]]),
                                in_=nonce_sb[0:1, 0:1]).then_inc(xsem, 16)
                    g.wait_ge(xsem, 48)
                    g.reg_mov(r_poff, 1)
                    g.reg_sub(r_poff, r_poff, r_slot)
                    g.reg_mul(r_poff, r_poff, 16)
                    g.reg_load(r_non, nonce_sb[0:1, 0:1])
                    g.reg_mov(r_iend, 500000)
                    g.reg_mov(r_fc, 48)
                    with g.Fori(0, r_iend):
                        g.dma_start(out=flag_sb[0:1, 0:1],
                                    in_=bass.AP(shF_t, r_poff,
                                                [[16, 1], [1, 1]])
                                    ).then_inc(xsem, 16)
                        g.reg_add(r_fc, r_fc, 16)
                        g.wait_ge(xsem, r_fc)
                        g.reg_load(r_tmp, flag_sb[0:1, 0:1])
                        g.reg_sub(r_tmp, r_tmp, r_non)
                        with g.If_eq(r_tmp, 0):
                            g.reg_mov(r_iend, 0)
                    g.reg_mov(r_tmp, 1)
                    g.reg_sub(r_tmp, r_tmp, r_slot)
                    g.reg_mul(r_tmp, r_tmp, 128 * DC * D)
                    g.dma_start(out=S_sb[:],
                                in_=bass.AP(shS_t, r_tmp,
                                            [[DC * D, 128], [1, DC * D]])
                                ).then_inc(xsem, 16)
                    g.reg_mov(r_tmp, 1)
                    g.reg_sub(r_tmp, r_tmp, r_slot)
                    g.reg_mul(r_tmp, r_tmp, 128 * DC)
                    g.dma_start(out=ksum[:],
                                in_=bass.AP(shK_t, r_tmp,
                                            [[DC, 128], [1, DC]])
                                ).then_inc(xsem, 16)
                    g.reg_add(r_fc, r_fc, 32)
                    g.wait_ge(xsem, r_fc)
                    return g.memset(flag_sb[0:1, 1:2], 0)
            import os as _os
            if _os.environ.get("L0_NOX") == "1":
                def xchg_stub(g):
                    return g.memset(flag_sb[0:1, 1:2], 0)
                xv = P.emit("g", xchg_stub, inc="pool", inc_n=1)
            else:
                xv = P.emit("g", xchg, inc="pool", inc_n=1)
            for dk in range(DC):
                P.v(lambda v, dk=dk: v.tensor_scalar_mul(
                    S_sb[:, dk * D:(dk + 1) * D],
                    S_sb[:, dk * D:(dk + 1) * D], hmask_s[:, 0:1]),
                    waits=[("pool", xv)] if dk == 0 else [])
            P.v(lambda v: v.tensor_scalar_mul(kdelta[:], ksum[:],
                                              hmask_s[:, 0:1]))
            P.v(lambda v: v.tensor_scalar_mul(kwork_bf2[:, 0:DC], ksum[:],
                                              hmask_s[:, 0:1]))
            P.barrier()

        if _L0UP < 3:
            P.emit("y", None, waits=[("st", P.c["st"])])
            P.finalize(nc, block, sems)
            return nc
        with ExitStack() as p2ctx:
            (w0s, S_stage, sadd) = _cm(
                p2ctx,
                nc.sbuf_tensor("w0s", [128, DC * D], BF16),
                nc.sbuf_tensor("S_stage", [128, 2 * DC * 128], BF16),
                nc.sbuf_tensor("sadd", [128, 4 * D], F32),
            )
            swork = S_sb
            w0s_act = [0, 0]
            for ic in range(DC):
                hh = ic % 2
                lv = P.emit("g", lambda g, ic=ic, hh=hh: g.dma_start(
                    out=sadd[:, hh * D:(hh + 1) * D],
                    in_=w0t_in[ic * 128:(ic + 1) * 128, :]),
                    inc="wld", inc_n=16,
                    waits=[("act", w0s_act[hh])] if w0s_act[hh] else [])
                w0s_act[hh] = P.s(lambda s, ic=ic, hh=hh: s.activation(
                    w0s[:, ic * D:(ic + 1) * D], sadd[:, hh * D:(hh + 1) * D],
                    AF.Sign), waits=[("wld", lv)])
            w0s_done = max(w0s_act)

            if _L0UP < 31:
                P.emit("y", None, waits=[("st", P.c["st"]),
                                         ("act", P.c["act"])])
                P.finalize(nc, block, sems)
                return nc
            def ld_xn(gidx):
                P.emit("y", lambda y, gidx=gidx: y.dma_start(
                    out=xn[:],
                    in_=bass.AP(xnd_t, gidx * 128 * DC * T,
                                [[DC * T, 128], [1, DC * T]])),
                    inc="ld", inc_n=16)

            def ld_kfm(gidx):
                P.emit("y", lambda y, gidx=gidx: y.dma_start(
                    out=kfm[:],
                    in_=bass.AP(kfmd_t, gidx * 128 * DC * T,
                                [[DC * T, 128], [1, DC * T]])),
                    inc="ld", inc_n=16)

            def ld_vtm(gidx):
                P.emit("y", lambda y, gidx=gidx: y.dma_start(
                    out=vtm[:],
                    in_=bass.AP(vtmd_t, gidx * 128 * TC * D,
                                [[TC * D, 128], [1, TC * D]])),
                    inc="ld", inc_n=16)

            _PREF = int(_os0.environ.get("L0_PREF", "3"))
            for gidx in range(NG):
                lv = P.emit("y", lambda y, gidx=gidx: y.dma_start(
                    out=xg[:].rearrange("p (c t) -> p c t", t=T),
                    in_=x_in[:, gidx * T:(gidx + 1) * T].rearrange(
                        "(c p) t -> p c t", p=128)), inc="ld", inc_n=16)
                if gidx == 0 or _PREF < 3:
                    ld_xn(gidx)
                if gidx == 0 or _PREF < 1:
                    ld_kfm(gidx)
                if gidx == 0 or _PREF < 2:
                    ld_vtm(gidx)
                kslot = gidx % 2
                if gidx == 0:
                    kw_prev = kdelta[:]
                    last_dadd = 0
                else:
                    dvv = last_dadd
                    for hh in range(2):
                        w = [("dve", dvv)] if dvv else []
                        if gidx == 1:
                            w.append(("act", w0s_done))
                        av2 = P.emit("g", lambda g, hh=hh, gidx=gidx:
                                     g.dma_start(
                            out=sadd[:],
                            in_=bass.AP(spill_t,
                                        (gidx - 1) * 128 * DC * D + hh * 4 * D,
                                        [[DC * D, 128], [1, 4 * D]])),
                            inc="wld", inc_n=16, waits=w)
                        for qq in range(4):
                            last_dadd = P.v(
                                lambda v, hh=hh, qq=qq: v.tensor_tensor(
                                    swork[:, hh * 4 * D + qq * D:
                                          hh * 4 * D + (qq + 1) * D],
                                    swork[:, hh * 4 * D + qq * D:
                                          hh * 4 * D + (qq + 1) * D],
                                    sadd[:, qq * D:(qq + 1) * D], ALU.add),
                                waits=[("wld", av2)] if qq == 0 else [])
                        dvv = last_dadd
                    kv = P.emit("g", lambda g, gidx=gidx: g.dma_start(
                        out=kspl[:],
                        in_=bass.AP(kspill_t, (gidx - 1) * 128 * DC,
                                    [[DC, 128], [1, DC]])),
                        inc="wld", inc_n=16)
                    kwn = kwork2[:, kslot * DC:(kslot + 1) * DC]
                    P.v(lambda v, kslot=kslot, kw_prev=kw_prev:
                        v.tensor_tensor(
                            kwork_bf2[:, kslot * DC:(kslot + 1) * DC],
                            kw_prev, kspl[:], ALU.add), waits=[("wld", kv)])
                    P.v(lambda v, kwn=kwn, kw_prev=kw_prev: v.tensor_tensor(
                        kwn, kw_prev, kspl[:], ALU.add))
                    kw_prev = kwn
                lds = P.c["ld"]
                for eng in ("t", "v", "s"):
                    P.emit(eng, None, waits=[("ld", lds)])
                if _os0.environ.get("L0_DBG") == "3" and gidx == 2:
                    P.v(lambda v: v.tensor_copy(kspl[:],
                                                kwork_bf2[:, 0:DC]))
                    P.emit("y", lambda y: y.dma_start(
                        out=x2_out[256:384, 0:DC], in_=kspl[:]),
                        inc="st", inc_n=16, waits=[("dve", P.c["dve"])])
                    P.emit("y", lambda y: y.dma_start(
                        out=x2_out[0:128, 0:1024], in_=swork[:, 0:D]),
                        inc="st", inc_n=16)
                    P.barrier()

                if _L0UP < 320:
                    P.barrier()
                    continue
                proj_fm(qkvw, 3 * D, 0, qkvb_cs, 0, scratch, AF.Identity, xn)
                P.barrier()
                for oc in range(DC):
                    sl = slice(oc * T, (oc + 1) * T)
                    elu1(scratch[:, sl], q_bf[:, sl], T)
                P.barrier()

                if _L0UP < 330:
                    P.barrier()
                    continue
                for j in range(TC):
                    span = (TC - j) * 128
                    bank = j % 2

                    def mm(t, j=j, span=span, bank=bank):
                        r = None
                        for icc in range(DC):
                            r = t.matmul(
                                pb[bank][:, 0:span],
                                kfm[:, icc * T + j * 128:icc * T + (j + 1) * 128],
                                q_bf[:, icc * T + j * 128:icc * T + j * 128 + span],
                                start=(icc == 0), stop=(icc == DC - 1))
                        return r
                    pev = P.t(mm)
                    P.v(lambda v, j=j, bank=bank: v.tensor_tensor(
                        sc_sb[:, j * T:j * T + 128], pb[bank][:, 0:128],
                        triu[:], ALU.mult), waits=[("pe", pev)])
                    if span > 128:
                        P.v(lambda v, j=j, span=span, bank=bank: v.tensor_copy(
                            sc_sb[:, j * T + 128:j * T + span],
                            pb[bank][:, 128:span]))
                P.barrier()
                if gidx + 1 < NG and _PREF >= 1:
                    ld_kfm(gidx + 1)

                for j in range(TC):
                    span = (TC - j) * 128
                    P.t(lambda t, j=j, span=span: t.matmul(
                        pb[2][0:1, j * 128:512], onesc_bf[:],
                        sc_sb[:, j * T:j * T + span],
                        start=(j == 0), stop=False))
                for dk in range(DC):
                    P.t(lambda t, dk=dk, kslot=kslot: t.matmul(
                        pb[2][0:1, 0:T],
                        kwork_bf2[:, kslot * DC + dk:kslot * DC + dk + 1],
                        q_bf[:, dk * T:(dk + 1) * T],
                        start=False, stop=(dk == DC - 1)))
                pev = P.c["pe"]
                dv = P.v(lambda v: v.tensor_scalar_add(
                    rowbuf[0:1, 0:T], pb[2][0:1, 0:T], DEN_EPS),
                    waits=[("pe", pev)])
                dv = P.v(lambda v: v.reciprocal(rowbuf[0:1, T:2 * T],
                                                rowbuf[0:1, 0:T]))
                pev = P.t(lambda t: t.matmul(pb[3][:, 0:T], onesr[:],
                                             rowbuf[0:1, T:2 * T],
                                             start=True, stop=True),
                          waits=[("dve", dv)])
                P.v(lambda v: v.tensor_copy(scratch[:, 0:T], pb[3][:, 0:T]),
                    waits=[("pe", pev)])
                P.barrier()

                if _L0UP < 340:
                    P.barrier()
                    continue
                evac = [0, 0]
                pe_hist = [0, 0]
                for dvc in range(DC):
                    bank = 4 + dvc % 2
                    stg = dvc % 2
                    last = 0
                    for dk in range(DC):
                        w = [("pe", pe_hist[stg])] if dk == 0 else []
                        last = P.v(lambda v, dk=dk, dvc=dvc, stg=stg:
                                   v.tensor_copy(
                            S_stage[:, stg * D + dk * 128:stg * D + (dk + 1) * 128],
                            swork[:, dk * D + dvc * 128:dk * D + (dvc + 1) * 128]),
                            waits=w)

                    def mm(t, dvc=dvc, bank=bank, stg=stg):
                        for dk in range(DC):
                            t.matmul(pb[bank][:, 0:T],
                                     S_stage[:, stg * D + dk * 128:stg * D + (dk + 1) * 128],
                                     q_bf[:, dk * T:(dk + 1) * T],
                                     start=(dk == 0), stop=False)
                        r = None
                        for j in range(TC):
                            span = (TC - j) * 128
                            r = t.matmul(
                                pb[bank][:, j * 128:512],
                                vtm[:, j * D + dvc * 128:j * D + (dvc + 1) * 128],
                                sc_sb[:, j * T:j * T + span],
                                start=False, stop=(j == TC - 1))
                        return r
                    pev = P.t(mm, waits=[("dve", last),
                                         ("dve", evac[dvc % 2])])
                    pe_hist[stg] = pev
                    P.v(lambda v, dvc=dvc, bank=bank: v.tensor_tensor(
                        tmp1[:], pb[bank][:, 0:T], scratch[:, 0:T], ALU.mult),
                        waits=[("pe", pev)])
                    evac[dvc % 2] = P.v(
                        lambda v, dvc=dvc: v.tensor_tensor(
                            xg[:, dvc * T:(dvc + 1) * T],
                            xg[:, dvc * T:(dvc + 1) * T], tmp1[:], ALU.add))
                P.barrier()
                if gidx + 1 < NG and _PREF >= 2:
                    ld_vtm(gidx + 1)

                if _L0UP < 350:
                    P.barrier()
                    continue
                ln_fm(xg, scratch, xn, gb=None)
                P.barrier()
                evac = [0, 0]
                for oc in range(DC):
                    bank = oc % 2

                    def mm(t, oc=oc, bank=bank):
                        r = None
                        for icc in range(DC):
                            r = t.matmul(
                                pb[bank][:, 0:T],
                                w0s[:, icc * D + oc * 128:icc * D + (oc + 1) * 128],
                                xn[:, icc * T:(icc + 1) * T],
                                start=(icc == 0), stop=(icc == DC - 1))
                        return r
                    ws = ([("act", w0s_done)]
                          if (gidx == 0 and oc < 2) else [])
                    pev = P.t(mm, waits=[("dve", evac[bank])] + ws)
                    evac[bank] = P.v(
                        lambda v, oc=oc, bank=bank: v.tensor_tensor(
                            xg[:, oc * T:(oc + 1) * T],
                            scratch[:, oc * T:(oc + 1) * T],
                            pb[bank][:, 0:T], ALU.add), waits=[("pe", pev)])
                P.barrier()
                if gidx + 1 < NG and _PREF >= 3:
                    ld_xn(gidx + 1)
                ln_fm(xg, scratch, None, gb=None)
                P.barrier()
                def st(y, gidx=gidx):
                    return y.dma_start(
                        out=x2_out[:, gidx * T:(gidx + 1) * T].rearrange(
                            "(c p) t -> p c t", p=128),
                        in_=scratch[:].rearrange("p (c t) -> p c t", t=T))
                P.emit("y", st, inc="st", inc_n=16,
                       waits=[("dve", P.c["dve"])])
                P.barrier()
                for eng in ("t", "v", "s"):
                    P.emit(eng, None, waits=[("st", P.c["st"])])

        P.emit("y", None, waits=[("st", P.c["st"])])

        P.finalize(nc, block, sems)
    return nc



def _cols(vec, p=128):
    return np.ascontiguousarray(vec.reshape(-1, p).T)


_L0_CACHE = {}


def _l0_inmaps(inp):
    x = np.asarray(inp["L0_attn_x"], dtype=np.float32)
    qkv_w = np.asarray(inp["L0_attn_qkv_w"], np.float32)
    qkv_b = np.asarray(inp["L0_attn_qkv_b"], np.float32)
    gate_w = np.asarray(inp["L0_attn_gate_w"], np.float32)
    gate_b = np.asarray(inp["L0_attn_gate_b"], np.float32)
    ln_g = np.asarray(inp["L0_attn_ln_g"], np.float32)
    ln_b = np.asarray(inp["L0_attn_ln_b"], np.float32)
    w0 = np.asarray(inp["L0_ffn_weight"], np.float32)
    w0t = np.ascontiguousarray(w0.T)

    ident16 = np.eye(128, dtype=BF)
    triu = np.triu(np.ones((128, 128))).astype(BF)
    onesr = np.ones((1, 128), np.float32)
    cpool = np.zeros((128, 8), np.float32)
    cpool[:, 0] = -0.5
    cpool[:, 1] = LN_EPS
    cpool[:, 2] = DEN_EPS
    cpool[:, 3] = 1.0

    nonce = np.full((1, 4), np.random.randint(1, 2 ** 30), np.int32)
    maps = []
    for c in range(8):
        b, h = c // 2, c % 2
        xs = np.ascontiguousarray(x[b, h * TOK:(h + 1) * TOK, :].T)
        hm = np.full((128, 1), float(h), np.float32)
        maps.append({
            "x_fm": xs, "nonce": nonce, "qkv_w": qkv_w, "gate_w": gate_w,
            "w0t": w0t, "qkv_b_c": _cols(qkv_b),
            "qkv_b_r": qkv_b.reshape(1, -1), "gate_b_c": _cols(gate_b),
            "ln_g_c": _cols(ln_g), "ln_b_c": _cols(ln_b), "hmask": hm,
            "ident16": ident16, "triu": triu, "ones_r": onesr,
            "cpool": cpool,
        })
    return maps


def run_l0(inp):
    if "l0" not in _L0_CACHE:
        _L0_CACHE["l0"] = build_l0()
    nc = _L0_CACHE["l0"]
    res = run_bass_kernel_spmd(nc, _l0_inmaps(inp), core_ids=list(range(8)))
    return [r["x2"] for r in res.results]


def build_l1():
    import os
    _UPTO = int(os.environ.get("L1_UPTO", "9999"))
    nc = bass.Bass("TRN2", target_bir_lowering=False)

    y0_in = nc.dram_tensor("y0_fm", [D, TOK], F32, kind="ExternalInput").ap()
    qw_in = nc.dram_tensor("qw", [D, D], F32, kind="ExternalInput").ap()
    kw_in = nc.dram_tensor("kw", [D, D], F32, kind="ExternalInput").ap()
    vw_in = nc.dram_tensor("vw", [D, D], F32, kind="ExternalInput").ap()
    ow_in = nc.dram_tensor("ow", [D, D], F32, kind="ExternalInput").ap()
    qb_in = nc.dram_tensor("qb_c", [128, 8], F32, kind="ExternalInput").ap()
    kb_in = nc.dram_tensor("kb_c", [128, 8], F32, kind="ExternalInput").ap()
    vb_in = nc.dram_tensor("vb_c", [128, 8], F32, kind="ExternalInput").ap()
    ob_in = nc.dram_tensor("ob_c", [128, 8], F32, kind="ExternalInput").ap()
    bk_in = nc.dram_tensor("bk_c", [128, 8], F32, kind="ExternalInput").ap()
    bv_in = nc.dram_tensor("bv_c", [128, 8], F32, kind="ExternalInput").ap()
    rel_in = nc.dram_tensor("rel_emb", [32, 16], F32, kind="ExternalInput").ap()
    grw_in = nc.dram_tensor("grep_w", [64, 8], F32, kind="ExternalInput").ap()
    grb_in = nc.dram_tensor("grep_b", [1, 8], F32, kind="ExternalInput").ap()
    gra_in = nc.dram_tensor("grep_a", [16, 1], F32, kind="ExternalInput").ap()
    onehot_in = nc.dram_tensor("onehot", [32, 20], F32, kind="ExternalInput").ap()
    P2_in = nc.dram_tensor("P2", [64, 128], F32, kind="ExternalInput").ap()
    Eb_in = nc.dram_tensor("E_big", [128, 16 * DC], BF16, kind="ExternalInput").ap()
    selRed_in = nc.dram_tensor("selRed", [128, 256], BF16, kind="ExternalInput").ap()
    selB_in = nc.dram_tensor("selB", [16, 1024], BF16, kind="ExternalInput").ap()
    selG_in = nc.dram_tensor("selG", [16, 256], F32, kind="ExternalInput").ap()
    w1t_in = nc.dram_tensor("w1t", [32 * 128, D], F32, kind="ExternalInput").ap()
    w2t_in = nc.dram_tensor("w2t", [32 * 128, D], F32, kind="ExternalInput").ap()
    h1t_in = nc.dram_tensor("h1t", [32 * 128, 4096], F32, kind="ExternalInput").ap()
    h2t_in = nc.dram_tensor("h2t", [32 * 128, 4096], F32, kind="ExternalInput").ap()
    w3t_in = nc.dram_tensor("w3t", [8 * 128, 4096], F32, kind="ExternalInput").ap()
    w1s_in = nc.dram_tensor("w1s_c", [128, 32], F32, kind="ExternalInput").ap()
    w2s_in = nc.dram_tensor("w2s_c", [128, 32], F32, kind="ExternalInput").ap()
    h1s_in = nc.dram_tensor("h1s_c", [128, 32], F32, kind="ExternalInput").ap()
    h2s_in = nc.dram_tensor("h2s_c", [128, 32], F32, kind="ExternalInput").ap()
    w3s_in = nc.dram_tensor("w3s_c", [128, 8], F32, kind="ExternalInput").ap()
    onesr_in = nc.dram_tensor("ones_r", [1, 128], F32, kind="ExternalInput").ap()
    cpool_in = nc.dram_tensor("cpool", [128, 8], F32, kind="ExternalInput").ap()

    out_dram = nc.dram_tensor("out_fm", [D, TOK], F32, kind="ExternalOutput").ap()
    y1_dbg = nc.dram_tensor("y1_dbg", [D, TOK], F32, kind="ExternalOutput").ap()
    y1_dram = nc.dram_tensor("y1_scr", [D, TOK], F32).ap()

    TA = 1024
    PH = 256

    from contextlib import ExitStack
    ctx = ExitStack()
    with ctx:
        (qbc, kbc, vbc, obc, bkc, bvc, bkc_bf, bvc_bf, rel_s, grw_s, grb_s,
         gra_s, onehot_s, P2_s, Eb_s, selRed_s, selB_s, selG_s, bkc_sw,
         bvc_sw, onesr, onesc, cpool, gwcol,
         Gmat, pb_sb, sgb, w1s, w2s, h1s, h2s, w3s, rowbuf, tmp1) = _cm(
            ctx,
            nc.sbuf_tensor("qbc", [128, 8], F32),
            nc.sbuf_tensor("kbc", [128, 8], F32),
            nc.sbuf_tensor("vbc", [128, 8], F32),
            nc.sbuf_tensor("obc", [128, 8], F32),
            nc.sbuf_tensor("bkc", [128, 8], F32),
            nc.sbuf_tensor("bvc", [128, 8], F32),
            nc.sbuf_tensor("bkc_bf", [128, 8], BF16),
            nc.sbuf_tensor("bvc_bf", [128, 8], BF16),
            nc.sbuf_tensor("rel_s", [32, 16], F32),
            nc.sbuf_tensor("grw_s", [64, 8], F32),
            nc.sbuf_tensor("grb_s", [1, 8], F32),
            nc.sbuf_tensor("gra_s", [16, 1], F32),
            nc.sbuf_tensor("onehot_s", [32, 20], F32),
            nc.sbuf_tensor("P2_s", [64, 128], F32),
            nc.sbuf_tensor("Eb_s", [128, 16 * DC], BF16),
            nc.sbuf_tensor("selRed_s", [128, 256], BF16),
            nc.sbuf_tensor("selB_s", [16, 1024], BF16),
            nc.sbuf_tensor("selG_s", [16, 256], F32),
            nc.sbuf_tensor("bkc_sw", [128, 8], BF16),
            nc.sbuf_tensor("bvc_sw", [128, 8], BF16),
            nc.sbuf_tensor("onesr_s", [1, 128], F32),
            nc.sbuf_tensor("onesc_s", [128, 1], F32),
            nc.sbuf_tensor("cpool_s", [128, 8], F32),
            nc.sbuf_tensor("gwcol", [128, 1], F32),
            nc.sbuf_tensor("Gmat", [128, 16 * DC], BF16),
            nc.sbuf_tensor("pb_sb", [16, 20], F32),
            nc.sbuf_tensor("sgb", [16, 1], F32),
            nc.sbuf_tensor("w1s_s", [128, 32], F32),
            nc.sbuf_tensor("w2s_s", [128, 32], F32),
            nc.sbuf_tensor("h1s_s", [128, 32], F32),
            nc.sbuf_tensor("h2s_s", [128, 32], F32),
            nc.sbuf_tensor("w3s_s", [128, 8], F32),
            nc.sbuf_tensor("rowbuf_1", [1, 3 * T], F32),
            nc.sbuf_tensor("tmp1_1", [128, T], F32),
        )
        pb = _cm(ctx, *[nc.psum_tensor(f"pb{i}", [128, 512], F32)
                        for i in range(8)])
        semh = _cm(ctx, *[nc.semaphore(f"sm_{n}") for n in CTRS])
        sems = dict(zip(CTRS, semh))
        block = ctx.enter_context(nc.Block())
        P = Prog()

        def ld(dst, src):
            P.emit("y", lambda y, d=dst, s=src: y.dma_start(out=d, in_=s),
                   inc="ld", inc_n=16)

        for dst, srcm in ((qbc[:], qb_in), (kbc[:], kb_in), (vbc[:], vb_in),
                          (obc[:], ob_in), (bkc[:], bk_in), (bvc[:], bv_in),
                          (rel_s[:], rel_in), (grw_s[:], grw_in),
                          (grb_s[:], grb_in), (gra_s[:], gra_in),
                          (onehot_s[:], onehot_in), (P2_s[:], P2_in),
                          (onesr[:], onesr_in), (cpool[:], cpool_in),
                          (w1s[:], w1s_in), (w2s[:], w2s_in),
                          (h1s[:], h1s_in), (h2s[:], h2s_in),
                          (w3s[:], w3s_in)):
            ld(dst, srcm)
        P.emit("g", lambda g: g.dma_start(out=Eb_s[:], in_=Eb_in),
               inc="wld", inc_n=16)
        P.emit("g", lambda g: g.dma_start(out=selRed_s[:], in_=selRed_in),
               inc="wld", inc_n=16)
        P.emit("g", lambda g: g.dma_start(out=selB_s[:], in_=selB_in),
               inc="wld", inc_n=16)
        ldv, wldv = P.c["ld"], P.c["wld"]
        for eng in ("t", "v", "s"):
            P.emit(eng, None, waits=[("ld", ldv), ("wld", wldv)])
        ld(selG_s[:], selG_in)
        P.v(lambda v: v.memset(onesc[:], 1.0))
        P.v(lambda v: v.tensor_copy(bkc_bf[:], bkc[:]))
        P.v(lambda v: v.tensor_copy(bvc_bf[:], bvc[:]))
        dvb = P.c["dve"]
        P.emit("y", lambda y: y.dma_start(out=bkc_sw[0:64, :],
                                          in_=bkc_bf[64:128, :]),
               inc="ld", inc_n=16, waits=[("dve", dvb)])
        P.emit("y", lambda y: y.dma_start(out=bkc_sw[64:128, :],
                                          in_=bkc_bf[0:64, :]),
               inc="ld", inc_n=16)
        P.emit("y", lambda y: y.dma_start(out=bvc_sw[0:64, :],
                                          in_=bvc_bf[64:128, :]),
               inc="ld", inc_n=16)
        P.emit("y", lambda y: y.dma_start(out=bvc_sw[64:128, :],
                                          in_=bvc_bf[0:64, :]),
               inc="ld", inc_n=16)
        dv = P.v(lambda v: v.tensor_reduce(gwcol[0:64, 0:1], grw_s[:],
                                           AX.X, ALU.add))
        pev = P.t(lambda t: t.matmul(pb[0][:, 0:1], P2_s[:], gwcol[0:64, 0:1],
                                     start=True, stop=True),
                  waits=[("dve", dv)])
        dv = P.v(lambda v: v.tensor_copy(gwcol[:], pb[0][:, 0:1]),
                 waits=[("pe", pev)])
        for c in range(DC):
            P.v(lambda v, c=c: v.tensor_scalar_mul(
                Gmat[:, c * 16:(c + 1) * 16], Eb_s[:, c * 16:(c + 1) * 16],
                gwcol[:, 0:1]))
        pev = P.t(lambda t: t.matmul(pb[1][0:16, 0:20], rel_s[:], onehot_s[:],
                                     start=True, stop=True))
        P.v(lambda v: v.tensor_copy(pb_sb[:], pb[1][0:16, 0:20]),
            waits=[("pe", pev)])
        dv = P.v(lambda v: v.tensor_reduce(rowbuf[0:1, 0:1], grb_s[:], AX.X,
                                           ALU.add))
        pev = P.t(lambda t: t.matmul(pb[2][0:16, 0:1], onesr[0:1, 0:16],
                                     rowbuf[0:1, 0:1], start=True, stop=True),
                  waits=[("dve", dv)])
        P.v(lambda v: v.tensor_copy(sgb[:], pb[2][0:16, 0:1]),
            waits=[("pe", pev)])
        P.barrier()

        def ln_fm_at(src_sb, col0, dst_sb, dcol0, T_=T):
            sw = src_sb.shape[1] // DC
            dw = dst_sb.shape[1] // DC
            prev_pe = 0
            for icc in range(DC):
                sl = slice(col0 + icc * sw, col0 + icc * sw + T_)
                av = P.s(lambda s, sl=sl: s.activation(
                    tmp1[:, 0:T_], src_sb[:, sl], AF.Square),
                    waits=[("pe", prev_pe)] if prev_pe else [])
                P.t(lambda t, sl=sl, icc=icc: t.matmul(
                    pb[6][0:1, 0:T_], onesc[:], src_sb[:, sl],
                    start=(icc == 0), stop=(icc == DC - 1)))
                prev_pe = P.t(lambda t, icc=icc: t.matmul(
                    pb[7][0:1, 0:T_], onesc[:], tmp1[:, 0:T_],
                    start=(icc == 0), stop=(icc == DC - 1)),
                    waits=[("act", av)])
            P.v(lambda v: v.tensor_scalar_mul(rowbuf[0:1, 0:T_],
                                              pb[6][0:1, 0:T_], -1.0 / D),
                waits=[("pe", prev_pe)])
            P.v(lambda v: v.tensor_scalar_mul(rowbuf[0:1, 2 * T_:3 * T_],
                                              pb[7][0:1, 0:T_], 1.0 / D))
            P.v(lambda v: v.tensor_tensor(rowbuf[0:1, T_:2 * T_],
                                          rowbuf[0:1, 0:T_],
                                          rowbuf[0:1, 0:T_], ALU.mult))
            dv = P.v(lambda v: v.tensor_tensor(
                rowbuf[0:1, 2 * T_:3 * T_], rowbuf[0:1, 2 * T_:3 * T_],
                rowbuf[0:1, T_:2 * T_], ALU.subtract))
            P.s(lambda s: s.activation(rowbuf[0:1, T_:2 * T_],
                                       rowbuf[0:1, 2 * T_:3 * T_], AF.Ln,
                                       bias=cpool[0:1, 1:2]),
                waits=[("dve", dv)])
            av = P.s(lambda s: s.activation(rowbuf[0:1, 2 * T_:3 * T_],
                                            rowbuf[0:1, T_:2 * T_], AF.Exp,
                                            scale=cpool[0:1, 0:1]))
            P.t(lambda t: t.matmul(pb[5][:, 0:T_], onesr[0:1, 0:128],
                                   rowbuf[0:1, 0:T_], start=True, stop=True),
                waits=[("dve", dv)])
            pev = P.t(lambda t: t.matmul(pb[7][:, 0:T_], onesr[0:1, 0:128],
                                         rowbuf[0:1, 2 * T_:3 * T_],
                                         start=True, stop=True),
                      waits=[("act", av)])
            for icc in range(DC):
                ssl = slice(col0 + icc * sw, col0 + icc * sw + T_)
                dsl = slice(dcol0 + icc * dw, dcol0 + icc * dw + T_)
                w = [("pe", pev)] if icc == 0 else []
                P.v(lambda v, ssl=ssl: v.scalar_tensor_tensor(
                    tmp1[:, 0:T_], src_sb[:, ssl], 1.0, pb[5][:, 0:T_],
                    ALU.mult, ALU.add), waits=w)
                P.v(lambda v, dsl=dsl: v.tensor_tensor(
                    dst_sb[:, dsl], tmp1[:, 0:T_], pb[7][:, 0:T_], ALU.mult))

        with ExitStack() as actx:
            (wslot, x2h, x2bh, q_sb, k_sb, k_swap, acc,
             prod, pbc, gate_sb, gate_r, sc0, gp, probs, y1g, x2t) = _cm(
                actx,
                nc.sbuf_tensor("wslot", [128, DC * D], BF16),
                nc.sbuf_tensor("x2h", [128, DC * TA], F32),
                nc.sbuf_tensor("x2bh", [128, DC * TA], BF16),
                nc.sbuf_tensor("q_sb", [128, DC * TA], BF16),
                nc.sbuf_tensor("k_sb", [128, DC * TA], BF16),
                nc.sbuf_tensor("k_swap", [128, DC * TA], BF16),
                nc.sbuf_tensor("acc", [128, DC * TA], BF16),
                nc.sbuf_tensor("prod", [128, 4 * PH], BF16),
                nc.sbuf_tensor("pbc", [128, 4 * PH], BF16),
                nc.sbuf_tensor("gate_sb", [16, TA], F32),
                nc.sbuf_tensor("gate_r", [16, TA], F32),
                nc.sbuf_tensor("sc0", [16, SRC * TA], F32),
                nc.sbuf_tensor("gp", [16, SRC * PH], F32),
                nc.sbuf_tensor("probs", [16, SRC * TA], BF16),
                nc.sbuf_tensor("y1g", [128, DC * T], F32),
                nc.sbuf_tensor("x2t", [128, T], F32),
            )

            def load_w(wsrc):
                lv = P.emit("g", lambda g, s=wsrc: g.dma_start(
                    out=wslot[:].rearrange("p (c n) -> p c n", c=DC),
                    in_=s.rearrange("(c p) n -> p c n", p=128)),
                    inc="wld", inc_n=16, waits=[("pe", P.c["pe"])])
                for eng in ("t",):
                    P.emit(eng, None, waits=[("wld", lv)])

            for ph in range(2 if _UPTO >= 500 else 1):
                for t_ in range(BATCH):
                    def ldx(y, ph=ph, t_=t_):
                        return y.dma_start(
                            out=bass.AP(x2h, t_ * PH,
                                        [[DC * TA, 128], [TA, DC], [1, PH]]),
                            in_=bass.AP(y0_in.tensor, t_ * T + ph * PH,
                                        [[TOK, 128], [128 * TOK, DC],
                                         [1, PH]]))
                    lv = P.emit("y", ldx, inc="ld", inc_n=16)
                for eng in ("t", "v", "s"):
                    P.emit(eng, None, waits=[("ld", lv)])
                for icc in range(DC):
                    P.v(lambda v, icc=icc: v.tensor_copy(
                        x2bh[:, icc * TA:(icc + 1) * TA],
                        x2h[:, icc * TA:(icc + 1) * TA]))
                P.barrier()

                def proj(bias, dst):
                    evac = [0, 0]
                    for oc in range(DC):
                        bank = oc % 2

                        def mm(t, oc=oc, bank=bank):
                            r = None
                            for icc in range(DC):
                                for tg in range(2):
                                    r = t.matmul(
                                        pb[bank + 2 * tg][:, 0:T],
                                        wslot[:, icc * D + oc * 128:
                                              icc * D + (oc + 1) * 128],
                                        x2bh[:, icc * TA + tg * T:
                                             icc * TA + (tg + 1) * T],
                                        start=(icc == 0), stop=(icc == DC - 1))
                            return r
                        pev = P.t(mm, waits=[("act", evac[bank])])
                        for tg in range(2):
                            evac[bank] = P.s(
                                lambda s, oc=oc, bank=bank, tg=tg, dst=dst,
                                bias=bias: s.activation(
                                    dst[:, oc * TA + tg * T:
                                        oc * TA + (tg + 1) * T],
                                    pb[bank + 2 * tg][:, 0:T], AF.Identity,
                                    bias=bias[:, oc:oc + 1]),
                                waits=[("pe", pev)])
                load_w(qw_in)
                proj(qbc, q_sb)
                load_w(kw_in)
                proj(kbc, k_sb)
                P.barrier()

                if _UPTO < 2:
                    continue
                def gmm(t):
                    r = None
                    for icc in range(DC):
                        for tg in range(2):
                            r = t.matmul(
                                pb[4 + tg][0:16, 0:T],
                                Gmat[:, icc * 16:(icc + 1) * 16],
                                q_sb[:, icc * TA + tg * T:icc * TA + (tg + 1) * T],
                                start=(icc == 0), stop=(icc == DC - 1))
                    return r
                pev = P.t(gmm)
                for tg in range(2):
                    av = P.s(lambda s, tg=tg: s.activation(
                        gate_sb[:, tg * T:(tg + 1) * T], pb[4 + tg][0:16, 0:T],
                        AF.Sigmoid, bias=sgb[:, 0:1]), waits=[("pe", pev)])
                for t2 in range(4):
                    def pmm(t, t2=t2):
                        r = None
                        for tt in range(4):
                            r = t.matmul(
                                pb[6][0:16, 0:PH],
                                selG_s[0:16, (t2 * 4 + tt) * 16:
                                       (t2 * 4 + tt + 1) * 16],
                                gate_sb[0:16, tt * PH:(tt + 1) * PH],
                                start=(tt == 0), stop=(tt == 3))
                        return r
                    pev = P.t(pmm, waits=[("act", P.c["act"]),
                                          ("dve", P.c["dve"])])
                    P.v(lambda v, t2=t2: v.tensor_copy(
                        gate_r[:, t2 * PH:(t2 + 1) * PH], pb[6][0:16, 0:PH]),
                        waits=[("pe", pev)])
                P.v(lambda v: v.tensor_scalar_mul(gate_r[:], gate_r[:],
                                                  gra_s[:, 0:1]))
                P.barrier()

                dvv = P.c["dve"]
                P.emit("y", lambda y: y.dma_start(
                    out=k_swap[0:64, :], in_=k_sb[64:128, :]),
                    inc="ld", inc_n=16, waits=[("dve", dvv)])
                klv = P.emit("y", lambda y: y.dma_start(
                    out=k_swap[64:128, :], in_=k_sb[0:64, :]),
                    inc="ld", inc_n=16)
                for eng in ("t", "v"):
                    P.emit(eng, None, waits=[("ld", klv)])

                def kv_ap(buf_nat, buf_sw, bias_nat, bias_sw, r, s2, half):
                    Gk = 5 * r + s2
                    sk, hk = Gk // 16, Gk % 16
                    if sk < 4:
                        buf = buf_nat if (hk % 2) == half else buf_sw
                        return buf[64 * half:64 * half + 64,
                                   (hk // 2) * TA + sk * PH:
                                   (hk // 2) * TA + (sk + 1) * PH]
                    col = bias_nat if (hk % 2) == half else bias_sw
                    return bass.AP(col, 64 * half * 8 + hk // 2,
                                   [[8, 64], [0, PH]])

                evac = [0, 0]
                pe_hist = [0, 0, 0, 0]
                for t2 in range(4):
                    u = t2 % 2
                    for s2 in range(SRC):
                        bank = (t2 * SRC + s2) % 2
                        pe_red = 0
                        for r in range(16):
                            G = 4 * r + t2
                            tq, hq = G // 16, G % 16
                            qap = q_sb[64 * u:64 * u + 64,
                                       (hq // 2) * TA + tq * PH:
                                       (hq // 2) * TA + (tq + 1) * PH]
                            kap = kv_ap(k_sb, k_swap, bkc_bf, bkc_sw,
                                        r, s2, u)
                            pg = r % 4
                            psl = slice(pg * PH, (pg + 1) * PH)
                            use_pool = (r % 3 == 2)
                            peng = P.g if use_pool else P.v
                            pctr = "pool" if use_pool else "dve"
                            dv = peng(lambda e, qap=qap, kap=kap, psl=psl,
                                      u=u: e.tensor_tensor(
                                prod[64 * u:64 * u + 64, psl], qap, kap,
                                ALU.mult),
                                waits=[("pe", pe_hist[pg])]
                                if pe_hist[pg] else [])
                            w = [(pctr, dv)]
                            if r == 0:
                                w.append(("act", evac[bank]))
                            pe_red = P.t(lambda t, r=r, bank=bank, psl=psl,
                                         u=u: t.matmul(
                                pb[bank][0:16, 0:PH],
                                selRed_s[64 * u:64 * u + 64,
                                         r * 16:(r + 1) * 16],
                                prod[64 * u:64 * u + 64, psl],
                                start=(r == 0), stop=(r == 15)),
                                waits=w)
                            pe_hist[pg] = pe_red
                        evac[bank] = P.s(
                            lambda s, t2=t2, s2=s2, bank=bank: s.copy(
                                sc0[:, (t2 * SRC + s2) * PH:
                                    (t2 * SRC + s2 + 1) * PH],
                                pb[bank][0:16, 0:PH]),
                            waits=[("pe", pe_red)])
                P.barrier()

                for t2 in range(BATCH):
                    gateap = bass.AP(gate_r, t2 * PH,
                                     [[TA, 16], [0, SRC], [1, PH]])
                    pbap = bass.AP(pb_sb, t2 * SRC,
                                   [[20, 16], [1, SRC], [0, PH]])
                    dv = P.v(lambda v, gateap=gateap, pbap=pbap:
                             v.tensor_tensor(
                        gp[:].rearrange("h (s o) -> h s o", s=SRC),
                        gateap, pbap, ALU.mult))
                    dv = P.v(lambda v, t2=t2: v.scalar_tensor_tensor(
                        sc0[:, t2 * SRC * PH:(t2 + 1) * SRC * PH],
                        sc0[:, t2 * SRC * PH:(t2 + 1) * SRC * PH],
                        1.0 / 256.0, gp[:], ALU.mult, ALU.add))
                s_sl = lambda j: bass.AP(sc0, j * PH,
                                         [[SRC * TA, 16], [SRC * PH, BATCH],
                                          [1, PH]])
                m_ap = gate_sb[:].rearrange("h (t o) -> h t o", t=BATCH)
                dv = P.v(lambda v: v.tensor_tensor(
                    m_ap, s_sl(0), s_sl(1), ALU.max))
                for j in (2, 3, 4):
                    dv = P.v(lambda v, j=j: v.tensor_tensor(
                        m_ap, m_ap, s_sl(j), ALU.max))
                map_b = bass.AP(gate_sb, 0, [[TA, 16], [PH, BATCH],
                                             [0, SRC], [1, PH]])
                dv = P.v(lambda v: v.tensor_tensor(
                    sc0[:].rearrange("h (t s o) -> h t s o", t=BATCH, s=SRC),
                    sc0[:].rearrange("h (t s o) -> h t s o", t=BATCH, s=SRC),
                    map_b, ALU.subtract))
                av = P.s(lambda s: s.activation(sc0[:], sc0[:], AF.Exp),
                         waits=[("dve", dv)])
                sum_ap = gp[:, 0:TA].rearrange("h (t o) -> h t o", t=BATCH)
                dv = P.v(lambda v: v.tensor_tensor(
                    sum_ap, s_sl(0), s_sl(1), ALU.add), waits=[("act", av)])
                for j in (2, 3, 4):
                    dv = P.v(lambda v, j=j: v.tensor_tensor(
                        sum_ap, sum_ap, s_sl(j), ALU.add))
                dv = P.v(lambda v: v.reciprocal(gp[:, 0:TA], gp[:, 0:TA]))
                rap = bass.AP(gp, 0, [[SRC * PH, 16], [PH, BATCH],
                                      [0, SRC], [1, PH]])
                dv = P.v(lambda v: v.tensor_tensor(
                    probs[:].rearrange("h (t s o) -> h t s o", t=BATCH,
                                       s=SRC),
                    sc0[:].rearrange("h (t s o) -> h t s o", t=BATCH, s=SRC),
                    rap, ALU.mult))
                P.barrier()

                load_w(vw_in)
                proj(vbc, k_sb)
                P.barrier()
                dvv = P.c["dve"]
                P.emit("y", lambda y: y.dma_start(
                    out=k_swap[0:64, :], in_=k_sb[64:128, :]),
                    inc="ld", inc_n=16, waits=[("dve", dvv),
                                               ("act", P.c["act"])])
                klv = P.emit("y", lambda y: y.dma_start(
                    out=k_swap[64:128, :], in_=k_sb[0:64, :]),
                    inc="ld", inc_n=16)
                for eng in ("t", "v"):
                    P.emit(eng, None, waits=[("ld", klv)])

                ev_s = [0, 0, 0, 0]
                ev_rd = [(None, 0)] * 4
                prod_rd = {2: 0, 3: 0}
                cnt = 0
                for t2 in range(BATCH):
                    for r in range(16):
                        half = r % 2
                        accsl = acc[64 * half:64 * half + 64,
                                    (r // 2) * TA + t2 * PH:
                                    (r // 2) * TA + (t2 + 1) * PH]
                        prev_mult = (None, 0)
                        prev_add = 0
                        for s2 in range(SRC):
                            bank = cnt % 4
                            slot = cnt % 4
                            cnt += 1
                            pev = P.t(lambda t, t2=t2, s2=s2, r=r, half=half,
                                      bank=bank: t.matmul(
                                pb[bank][64 * half:64 * half + 64, 0:PH],
                                selB_s[0:16, r * 64:(r + 1) * 64],
                                probs[:, (t2 * SRC + s2) * PH:
                                      (t2 * SRC + s2 + 1) * PH],
                                start=True, stop=True),
                                waits=[("act", ev_s[bank])])
                            sw = [("pe", pev)]
                            if ev_rd[slot][0] is not None:
                                sw.append(ev_rd[slot])
                            ev_s[bank] = P.s(
                                lambda s, bank=bank, half=half, slot=slot:
                                s.copy(
                                    pbc[64 * half:64 * half + 64,
                                        slot * PH:(slot + 1) * PH],
                                    pb[bank][64 * half:64 * half + 64, 0:PH]),
                                waits=sw)
                            vap = kv_ap(k_sb, k_swap, bvc_bf, bvc_sw,
                                        r, s2, half)
                            use_pool = s2 in (1, 3)
                            eng = P.g if use_pool else P.v
                            ctr = "pool" if use_pool else "dve"
                            if s2 == 0:
                                mv = P.v(
                                    lambda v, accsl=accsl, vap=vap,
                                    half=half, slot=slot: v.tensor_tensor(
                                        accsl,
                                        pbc[64 * half:64 * half + 64,
                                            slot * PH:(slot + 1) * PH],
                                        vap, ALU.mult),
                                    waits=[("act", ev_s[bank])])
                                ev_rd[slot] = ("dve", mv)
                                prev_add = mv
                            else:
                                pslot = 2 + s2 % 2
                                psl = slice(pslot * PH, (pslot + 1) * PH)
                                mw = [("act", ev_s[bank])]
                                if prod_rd[pslot]:
                                    mw.append(("dve", prod_rd[pslot]))
                                mv = eng(lambda e, vap=vap, half=half,
                                         slot=slot, psl=psl:
                                         e.tensor_tensor(
                                        prod[64 * half:64 * half + 64, psl],
                                        pbc[64 * half:64 * half + 64,
                                            slot * PH:(slot + 1) * PH],
                                        vap, ALU.mult), waits=mw)
                                ev_rd[slot] = (ctr, mv)
                                aw = [(ctr, mv)]
                                mv2 = P.v(
                                    lambda v, accsl=accsl, half=half,
                                    psl=psl: v.tensor_tensor(
                                        accsl, accsl,
                                        prod[64 * half:64 * half + 64, psl],
                                        ALU.add), waits=aw)
                                prod_rd[pslot] = mv2
                                prev_add = mv2
                            prev_mult = (ctr, mv)
                P.barrier()
                if _UPTO < 4:
                    continue
                load_w(ow_in)
                for tg in range(2):
                    prev_stt = [0, 0]
                    for oc in range(DC):
                        bank = oc % 2

                        def mm(t, oc=oc, bank=bank, tg=tg):
                            r = None
                            for icc in range(DC):
                                r = t.matmul(
                                    pb[bank][:, 0:T],
                                    wslot[:, icc * D + oc * 128:
                                          icc * D + (oc + 1) * 128],
                                    acc[:, icc * TA + tg * T:
                                        icc * TA + (tg + 1) * T],
                                    start=(icc == 0), stop=(icc == DC - 1))
                            return r
                        pev = P.t(mm, waits=[("dve", prev_stt[bank])])
                        ylv = P.emit("y", lambda y, oc=oc, tg=tg,
                                     ph=ph: y.dma_start(
                            out=x2t[:, 0:T].rearrange(
                                "p (a o) -> p a o", a=2),
                            in_=bass.AP(y0_in.tensor,
                                        oc * 128 * TOK + tg * 2 * T + ph * PH,
                                        [[TOK, 128], [T, 2], [1, PH]])),
                            inc="ld", inc_n=16,
                            waits=[("dve", P.c["dve"])])
                        av = P.s(lambda s, oc=oc, bank=bank: s.activation(
                            tmp1[:, 0:T], pb[bank][:, 0:T],
                            AF.Identity, bias=obc[:, oc:oc + 1]),
                            waits=[("pe", pev), ("dve", P.c["dve"])])
                        prev_stt[bank] = P.v(
                            lambda v, oc=oc: v.tensor_tensor(
                                y1g[:, oc * T:(oc + 1) * T], tmp1[:, 0:T],
                                x2t[:, 0:T], ALU.add),
                            waits=[("act", av), ("ld", ylv)])
                    P.barrier()
                    if _UPTO < 420:
                        continue
                    ln_fm_at(y1g, 0, y1g, 0, T_=T)
                    P.barrier()
                    if _UPTO < 430:
                        continue
                    for oc in range(DC):
                        P.emit("y", lambda y, oc=oc, tg=tg, ph=ph:
                               y.dma_start(
                            out=bass.AP(y1_dram.tensor,
                                        oc * 128 * TOK + tg * 2 * T + ph * PH,
                                        [[TOK, 128], [T, 2], [1, PH]]),
                            in_=y1g[:, oc * T:(oc + 1) * T].rearrange(
                                "p (a o) -> p a o", a=2)),
                            inc="st", inc_n=16,
                            waits=[("dve", P.c["dve"])] if oc == 0 else [])
                    P.barrier()

        P.barrier()

        with ExitStack() as bctx:
            if _UPTO < 600:
                bctx = bctx
            (a_sb, b_sb, a1, wbuf, y1f, ovl) = _cm(
                bctx,
                nc.sbuf_tensor("a_sb", [128, 32 * TA], BF16),
                nc.sbuf_tensor("b_sb", [128, 32 * TA], BF16),
                nc.sbuf_tensor("a1", [128, T], BF16),
                nc.sbuf_tensor("wbuf", [128, 2 * 32 * 128], BF16),
                nc.sbuf_tensor("y1f", [128, 2 * T], F32),
                nc.sbuf_tensor("ovl", [128, DC * TA], F32),
            )
            res = ovl
            y1bf = ovl[:].bitcast(BF16)[:, 0:DC * TA]
            for hf in range(2 if _UPTO >= 600 else 0):
                col0 = hf * TA
                lv = P.emit("g", lambda g, col0=col0: g.dma_start(
                    out=y1bf[:].rearrange("p (c n) -> p c n", c=DC),
                    in_=bass.AP(y1_dram.tensor, col0,
                                [[TOK, 128], [128 * TOK, DC], [1, TA]])),
                    inc="wld", inc_n=16)
                for eng in ("t", "v", "s", "y"):
                    P.emit(eng, None, waits=[("wld", lv)])

                WS = DC * 128
                pe_after = [0, 0]

                def ldA(op, slot):
                    w = [("pe", pe_after[slot])] if pe_after[slot] else []
                    lv = P.emit("g", lambda g, op=op, slot=slot: g.dma_start(
                        out=wbuf[:, slot * 2 * WS:slot * 2 * WS + WS],
                        in_=bass.AP(w1t_in.tensor, op * 128 * D,
                                    [[D, 128], [1, D]])),
                        inc="wld", inc_n=16, waits=w)
                    return P.emit("g", lambda g, op=op, slot=slot:
                                  g.dma_start(
                        out=wbuf[:, slot * 2 * WS + WS:(slot + 1) * 2 * WS],
                        in_=bass.AP(w2t_in.tensor, op * 128 * D,
                                    [[D, 128], [1, D]])),
                        inc="wld", inc_n=16)
                lvs = {0: ldA(0, 0)}
                bank_act = [0] * 4
                bank_dve = [0] * 4
                for op in range(32):
                    slot = op % 2
                    if op + 1 < 32:
                        lvs[op + 1] = ldA(op + 1, (op + 1) % 2)
                    b1s = (2 * slot, 2 * slot + 1)
                    b2s = (4 + 2 * slot, 4 + 2 * slot + 1)

                    def mm1(t, op=op, slot=slot, b1s=b1s, b2s=b2s):
                        r = None
                        for icc in range(DC):
                            for tg in range(2):
                                r = t.matmul(
                                    pb[b1s[tg]][:, 0:T],
                                    wbuf[:, slot * 2 * WS + icc * 128:
                                         slot * 2 * WS + (icc + 1) * 128],
                                    y1bf[:, icc * TA + tg * T:
                                         icc * TA + (tg + 1) * T],
                                    start=(icc == 0), stop=(icc == DC - 1))
                        for icc in range(DC):
                            for tg in range(2):
                                r = t.matmul(
                                    pb[b2s[tg]][:, 0:T],
                                    wbuf[:, slot * 2 * WS + WS + icc * 128:
                                         slot * 2 * WS + WS + (icc + 1) * 128],
                                    y1bf[:, icc * TA + tg * T:
                                         icc * TA + (tg + 1) * T],
                                    start=(icc == 0), stop=(icc == DC - 1))
                        return r
                    w = [("wld", lvs[op])]
                    for bk in b1s:
                        if bank_act[bk]:
                            w.append(("act", bank_act[bk]))
                    for bk in b2s:
                        if bank_dve[bk % 4]:
                            w.append(("dve", bank_dve[bk % 4]))
                    pev = P.t(mm1, waits=w)
                    for tg in range(2):
                        av = P.s(lambda s, op=op, tg=tg, b1=b1s[tg]:
                                 s.activation(
                            a1[:, 0:T], pb[b1][:, 0:T],
                            AF.Silu, scale=w1s[:, op:op + 1]),
                            waits=[("pe", pev), ("dve", P.c["dve"])])
                        bank_act[b1s[tg]] = av
                        bank_dve[b2s[tg] % 4] = P.v(
                            lambda v, op=op, tg=tg, b2=b2s[tg]:
                            v.scalar_tensor_tensor(
                                a_sb[:, op * TA + tg * T:
                                     op * TA + (tg + 1) * T],
                                pb[b2][:, 0:T], w2s[:, op:op + 1],
                                a1[:, 0:T], ALU.mult, ALU.mult),
                            waits=[("act", av)])
                    pe_after[slot] = P.c["pe"]
                P.barrier()

                for (src_sb, dst_sb, h_in, h_s) in (
                        (a_sb, b_sb, h1t_in, h1s),
                        (b_sb, a_sb, h2t_in, h2s)):
                    pe_after = [0, 0]

                    def ldH(op, slot, h_in=h_in):
                        w = [("pe", pe_after[slot])] if pe_after[slot] else \
                            [("pe", P.c["pe"])]
                        return P.emit("g", lambda g, op=op, slot=slot:
                                      g.dma_start(
                            out=wbuf[:, slot * 4096:(slot + 1) * 4096],
                            in_=bass.AP(h_in.tensor, op * 128 * 4096,
                                        [[4096, 128], [1, 4096]])),
                            inc="wld", inc_n=16, waits=w)
                    lvs = {0: ldH(0, 0)}
                    bank_act = [0] * 4
                    for op in range(32):
                        slot = op % 2
                        if op + 1 < 32:
                            lvs[op + 1] = ldH(op + 1, (op + 1) % 2)
                        bks = (2 * slot, 2 * slot + 1)

                        def mmh(t, op=op, slot=slot, src_sb=src_sb, bks=bks):
                            r = None
                            for icc in range(32):
                                for tg in range(2):
                                    r = t.matmul(
                                        pb[bks[tg]][:, 0:T],
                                        wbuf[:, slot * 4096 + icc * 128:
                                             slot * 4096 + (icc + 1) * 128],
                                        src_sb[:, icc * TA + tg * T:
                                               icc * TA + (tg + 1) * T],
                                        start=(icc == 0), stop=(icc == 31))
                            return r
                        w = [("wld", lvs[op])]
                        for bk in bks:
                            if bank_act[bk]:
                                w.append(("act", bank_act[bk]))
                        pev = P.t(mmh, waits=w)
                        for tg in range(2):
                            bank_act[bks[tg]] = P.s(
                                lambda s, op=op, tg=tg, dst_sb=dst_sb,
                                h_s=h_s, bk=bks[tg]: s.activation(
                                    dst_sb[:, op * TA + tg * T:
                                           op * TA + (tg + 1) * T],
                                    pb[bk][:, 0:T], AF.Gelu,
                                    scale=h_s[:, op:op + 1]),
                                waits=[("pe", pev)])
                        pe_after[slot] = P.c["pe"]
                    P.barrier()

                pe_after = [0, 0]

                def ldD(oc, slot):
                    w = [("pe", pe_after[slot])] if pe_after[slot] else \
                        [("pe", P.c["pe"])]
                    return P.emit("g", lambda g, oc=oc, slot=slot:
                                  g.dma_start(
                        out=wbuf[:, slot * 4096:(slot + 1) * 4096],
                        in_=bass.AP(w3t_in.tensor, oc * 128 * 4096,
                                    [[4096, 128], [1, 4096]])),
                        inc="wld", inc_n=16, waits=w)
                lvs = {0: ldD(0, 0)}
                bank_dve = [0] * 4
                for oc in range(DC):
                    slot = oc % 2
                    if oc + 1 < DC:
                        lvs[oc + 1] = ldD(oc + 1, (oc + 1) % 2)
                    bks = (2 * slot, 2 * slot + 1)
                    ylv = P.emit("y", lambda y, oc=oc, col0=col0:
                                 y.dma_start(
                        out=y1f[:].rearrange("p (a o) -> p a o", a=2),
                        in_=bass.AP(y1_dram.tensor,
                                    oc * 128 * TOK + col0,
                                    [[TOK, 128], [T, 2], [1, T]])),
                        inc="ld", inc_n=16,
                        waits=[("dve", P.c["dve"])])

                    def mm3(t, oc=oc, slot=slot, bks=bks):
                        r = None
                        for icc in range(32):
                            for tg in range(2):
                                r = t.matmul(
                                    pb[bks[tg]][:, 0:T],
                                    wbuf[:, slot * 4096 + icc * 128:
                                         slot * 4096 + (icc + 1) * 128],
                                    a_sb[:, icc * TA + tg * T:
                                         icc * TA + (tg + 1) * T],
                                    start=(icc == 0), stop=(icc == 31))
                        return r
                    w = [("wld", lvs[oc])]
                    for bk in bks:
                        if bank_dve[bk]:
                            w.append(("dve", bank_dve[bk]))
                    pev = P.t(mm3, waits=w)
                    for tg in range(2):
                        bank_dve[bks[tg]] = P.v(
                            lambda v, oc=oc, tg=tg, bk=bks[tg]:
                            v.scalar_tensor_tensor(
                                res[:, oc * TA + tg * T:
                                    oc * TA + (tg + 1) * T],
                                pb[bk][:, 0:T], w3s[:, oc:oc + 1],
                                y1f[:, tg * T:(tg + 1) * T], ALU.mult,
                                ALU.add),
                            waits=[("pe", pev), ("ld", ylv)])
                    pe_after[slot] = P.c["pe"]
                P.barrier()
                for tg in range(2):
                    ln_fm_at(res, tg * T, res, tg * T, T_=T)
                    P.barrier()
                def sto(y, col0=col0):
                    return y.dma_start(
                        out=bass.AP(out_dram.tensor, col0,
                                    [[TOK, 128], [128 * TOK, DC], [1, TA]]),
                        in_=res[:].rearrange("p (c n) -> p c n", c=DC))
                P.emit("y", sto, inc="st", inc_n=16,
                       waits=[("dve", P.c["dve"])])
                P.barrier()
                for eng in ("t", "v", "s"):
                    P.emit(eng, None, waits=[("st", P.c["st"])])

        P.emit("y", lambda y: y.dma_start(out=y1_dbg, in_=y1_dram),
               inc="st", inc_n=16)
        P.emit("y", None, waits=[("st", P.c["st"])])
        P.finalize(nc, block, sems)
    return nc


def _l1_inmaps(inp, x2_shards):
    qw = np.asarray(inp["L1_attn_q_w"], np.float32)
    kw = np.asarray(inp["L1_attn_k_w"], np.float32)
    vw = np.asarray(inp["L1_attn_v_w"], np.float32)
    ow = np.asarray(inp["L1_attn_out_w"], np.float32)
    rel = np.asarray(inp["L1_attn_rel_emb"], np.float32)
    grw = np.asarray(inp["L1_attn_grep_w"], np.float32)
    grb = np.asarray(inp["L1_attn_grep_b"], np.float32).reshape(1, 8)
    gra = np.asarray(inp["L1_attn_grep_a"], np.float32).reshape(16, 1)
    bk = np.asarray(inp["L1_attn_bias_k"], np.float32).reshape(-1)
    bv = np.asarray(inp["L1_attn_bias_v"], np.float32).reshape(-1)
    def _tileT(w):
        N, K = w.shape
        n_op, n_c = N // 128, K // 128
        a = w.reshape(n_op, 128, n_c, 128)
        a = a.transpose(0, 3, 2, 1)
        return np.ascontiguousarray(a).reshape(n_op * 128, K)

    w1t = _tileT(np.asarray(inp["L1_ffn_w1_w"], np.float32))
    w2t = _tileT(np.asarray(inp["L1_ffn_w2_w"], np.float32))
    h1t = _tileT(np.asarray(inp["L1_ffn_h1_w"], np.float32))
    h2t = _tileT(np.asarray(inp["L1_ffn_h2_w"], np.float32))
    w3t = _tileT(np.asarray(inp["L1_ffn_w3_w"], np.float32))

    onehot = np.zeros((32, 20), np.float32)
    for t_ in range(4):
        for sp in range(5):
            rp = sp - t_
            bkt = (16 if rp > 0 else 0) + abs(rp)
            onehot[bkt, t_ * 5 + sp] = 1.0
    P2 = np.zeros((64, 128), np.float32)
    for p in range(128):
        P2[p % 64, p] = 1.0
    Eb = np.zeros((128, 16 * DC), np.float32)
    for c in range(DC):
        for p in range(128):
            h = 2 * c + p // 64
            Eb[p, c * 16 + h] = 1.0
    selRed = np.zeros((128, 256), np.float32)
    for r in range(16):
        selRed[:, r * 16 + r] = 1.0
    selB = np.zeros((16, 1024), np.float32)
    for r in range(16):
        selB[r, r * 64:(r + 1) * 64] = 1.0
    selG = np.zeros((16, 256), np.float32)
    for t2 in range(4):
        for tt in range(4):
            for h in range(16):
                if h % 4 == t2:
                    selG[h, (t2 * 4 + tt) * 16 + 4 * tt + h // 4] = 1.0
    onesr = np.ones((1, 128), np.float32)
    cpool = np.zeros((128, 8), np.float32)
    cpool[:, 0] = -0.5
    cpool[:, 1] = LN_EPS
    cpool[:, 3] = 1.0

    base = {
        "qw": qw, "kw": kw, "vw": vw, "ow": ow,
        "qb_c": _cols(np.asarray(inp["L1_attn_q_b"], np.float32)),
        "kb_c": _cols(np.asarray(inp["L1_attn_k_b"], np.float32)),
        "vb_c": _cols(np.asarray(inp["L1_attn_v_b"], np.float32)),
        "ob_c": _cols(np.asarray(inp["L1_attn_out_b"], np.float32)),
        "bk_c": _cols(bk), "bv_c": _cols(bv),
        "rel_emb": rel, "grep_w": grw, "grep_b": grb, "grep_a": gra,
        "onehot": onehot, "P2": P2, "E_big": Eb.astype(BF),
        "selRed": selRed.astype(BF), "selB": selB.astype(BF), "selG": selG,
        "w1t": w1t, "w2t": w2t, "h1t": h1t, "h2t": h2t, "w3t": w3t,
        "w1s_c": _cols(np.asarray(inp["L1_ffn_w1_s"], np.float32)),
        "w2s_c": _cols(np.asarray(inp["L1_ffn_w2_s"], np.float32)),
        "h1s_c": _cols(np.asarray(inp["L1_ffn_h1_s"], np.float32)),
        "h2s_c": _cols(np.asarray(inp["L1_ffn_h2_s"], np.float32)),
        "w3s_c": _cols(np.asarray(inp["L1_ffn_w3_s"], np.float32)),
        "ones_r": onesr, "cpool": cpool,
    }
    maps = []
    for c in range(8):
        h, cc = c // 4, c % 4
        y0 = np.concatenate(
            [x2_shards[2 * t_ + h][:, 512 * cc:512 * (cc + 1)]
             for t_ in range(4)], axis=1)
        m = dict(base)
        m["y0_fm"] = np.ascontiguousarray(y0)
        maps.append(m)
    return maps


_L1_CACHE = {}


def run_l1(inp, x2_shards):
    if "l1" not in _L1_CACHE:
        _L1_CACHE["l1"] = build_l1()
    nc = _L1_CACHE["l1"]
    res = run_bass_kernel_spmd(nc, _l1_inmaps(inp, x2_shards),
                               core_ids=list(range(8)))
    return [r["out_fm"] for r in res.results], [r["y1_dbg"] for r in res.results]


def kernel(**inp):
    x2 = run_l0(inp)
    outs, _y1 = run_l1(inp, x2)
    final = np.empty((4, 4096, 1024), np.float32)
    for c in range(8):
        blk = outs[c].reshape(1024, 4, 512).transpose(1, 2, 0)
        final[:, 512 * c:512 * (c + 1), :] = blk
    return final
